# version 08
# baseline (speedup 1.0000x reference)
DMA_INC = 16


class LazyReg:

    def __init__(self, value):
        self.value = value
        self._reg = None

    def get(self, eng):
        if self._reg is None:
            self._reg = eng.to_reg(self.value)
        return self._reg


def _resolve_lazy(kwargs, eng):
    if any(isinstance(v, LazyReg) for v in kwargs.values()):
        return {k: (v.get(eng) if isinstance(v, LazyReg) else v) for k, v in kwargs.items()}
    return kwargs


class _Base:
    def __init__(self, fw, name):
        self.fw = fw
        self.name = name
        self.sem = None
        self.n = 0

    def at(self, m):
        return (self, m)

    def now(self):
        return (self, self.n)


class Q(_Base):

    def __init__(self, fw, name, engine_attr, serialize):
        super().__init__(fw, name)
        self.engine_attr = engine_attr
        self.serialize = serialize
        self.recs = []

    def op(self, fn, *args, wait=(), inc=True, **kwargs):
        self_wait = self.n if (self.serialize and self.n > 0) else None
        waits = [(s, m) for (s, m) in wait if m > 0]
        do_inc = 1 if inc else 0
        self.recs.append((self.fw.next_seq(), "op",
                          (fn, args, kwargs, waits, do_inc, self_wait)))
        self.n += do_inc
        return self.n

    def wait_only(self, wait):
        self.recs.append((self.fw.next_seq(), "op",
                          (None, (), {}, [(s, m) for (s, m) in wait if m > 0], 0, None)))
        return self.n


class DmaQ:

    def __init__(self, fw, name, engine_attr):
        self.fw = fw
        self.name = name
        self.engine_attr = engine_attr
        self.recs = []


class Stream(_Base):

    def __init__(self, fw, name):
        super().__init__(fw, name)
        self.outstanding_from = 0
        self.group_open = False

    def dma(self, q, out, in_, wait=(), new_group=True, fn="dma_start", **kwargs):
        waits = [(s, m) for (s, m) in wait if m > 0]
        if new_group:
            gate = self.n if self.n > 0 else None
            self.group_open = True
        else:
            gate = None
        self.fw._all_streams.add(self)
        q.recs.append((self.fw.next_seq(), "dma",
                       (fn, out, in_, kwargs, waits, self, gate)))
        self.n += DMA_INC
        return self.n

    def op(self, q, fn, *args, wait=(), new_group=True, **kwargs):
        waits = [(s, m) for (s, m) in wait if m > 0]
        gate = self.n if (new_group and self.n > 0) else None
        self.fw._all_streams.add(self)
        q.recs.append((self.fw.next_seq(), "dmaop",
                       (fn, args, kwargs, waits, self, gate)))
        self.n += DMA_INC
        return self.n


class Fw:
    def __init__(self, nc):
        self.nc = nc
        self._seq = 0
        self.pe = Q(self, "pe", "tensor", serialize=False)
        self.act = Q(self, "act", "scalar", serialize=True)
        self.dve = Q(self, "dve", "vector", serialize=True)
        self.pool = Q(self, "pool", "gpsimd", serialize=True)
        self.hd = DmaQ(self, "hd", "sync")
        self.hd2 = DmaQ(self, "hd2", "scalar")
        self.sd = DmaQ(self, "sd", "gpsimd")
        self.queues = [self.pe, self.act, self.dve, self.pool]
        self.dmaqs = [self.hd, self.hd2, self.sd]
        self._all_streams = set()

    def stream(self, name):
        return Stream(self, name)

    def next_seq(self):
        self._seq += 1
        return self._seq

    def finalize(self, es):
        nc = self.nc
        for q in self.queues:
            if q.recs:
                q.sem = es.enter_context(nc.semaphore("sem_" + q.name))
        for s in self._all_streams:
            s.sem = es.enter_context(nc.semaphore("sem_" + s.name))
        block = es.enter_context(nc.Block())

        by_engine = {}
        for q in self.queues:
            if q.recs:
                by_engine.setdefault(q.engine_attr, []).append(q)
        for q in self.dmaqs:
            if q.recs:
                by_engine.setdefault(q.engine_attr, []).append(q)

        def make_emitter(qs):
            def emitter(eng):
                recs = []
                for q in qs:
                    for rec in q.recs:
                        recs.append((rec[0], q, rec[1], rec[2]))
                recs.sort(key=lambda t: t[0])
                waited = {}

                def do_waits(ws):
                    for (s, m) in ws:
                        key = id(s)
                        if m > waited.get(key, 0):
                            eng.wait_ge(s.sem, m)
                            waited[key] = m

                for _, q, kind, payload in recs:
                    if kind == "op":
                        fn, args, kwargs, waits, do_inc, self_wait = payload
                        ws = list(waits)
                        if self_wait is not None:
                            ws.append((q, self_wait))
                        do_waits(ws)
                        if fn is None:
                            continue
                        inst = getattr(eng, fn)(*args, **kwargs)
                        if do_inc:
                            inst.then_inc(q.sem, 1)
                    elif kind == "dma":
                        fn, out, in_, kwargs, waits, stream, gate = payload
                        ws = list(waits)
                        if gate is not None:
                            ws.append((stream, gate))
                        do_waits(ws)
                        inst = getattr(eng, fn)(out=out, in_=in_, **kwargs)
                        inst.then_inc(stream.sem, DMA_INC)
                    else:
                        fn, args, kwargs, waits, stream, gate = payload
                        ws = list(waits)
                        if gate is not None:
                            ws.append((stream, gate))
                        do_waits(ws)
                        inst = getattr(eng, fn)(*args, **_resolve_lazy(kwargs, eng))
                        inst.then_inc(stream.sem, DMA_INC)
            return emitter

        for engine_attr, qs in by_engine.items():
            getattr(block, engine_attr)(make_emitter(qs))
        return block


class Buf:
    __slots__ = ("name", "lw", "lr")

    def __init__(self, name=""):
        self.name = name
        self.lw = None
        self.lr = []


def _collect_waits(q, reads, writes, extra):
    waits = []
    for b in reads:
        if b.lw is not None:
            waits.append(b.lw)
    for b in writes:
        if b.lw is not None:
            waits.append(b.lw)
        waits.extend(b.lr)
    waits.extend(extra)
    best = {}
    for (s, m) in waits:
        if s is q:
            continue
        if m > best.get(id(s), (None, -1))[1]:
            best[id(s)] = (s, m)
    return [v for v in best.values()]


def _update(reads, writes, mk):
    for b in reads:
        b.lr.append(mk)
    for b in writes:
        b.lw = mk
        b.lr = []


def E(q, fn, *args, reads=(), writes=(), extra_wait=(), inc=True, **kw):
    waits = _collect_waits(q, reads, writes, extra_wait)
    m = q.op(fn, *args, wait=waits, inc=inc, **kw)
    mk = (q, m if inc else q.n + 1)
    _update(reads, writes, mk)
    return m


def ED(stream, q, out, in_, reads=(), writes=(), extra_wait=(), new_group=True, **kw):
    waits = _collect_waits(stream, reads, writes, extra_wait)
    m = stream.dma(q, out, in_, wait=waits, new_group=new_group, **kw)
    mk = (stream, m)
    _update(reads, writes, mk)
    return m


def EDOP(stream, q, fn, *args, reads=(), writes=(), extra_wait=(), new_group=True, **kw):
    waits = _collect_waits(stream, reads, writes, extra_wait)
    m = stream.op(q, fn, *args, wait=waits, new_group=new_group, **kw)
    mk = (stream, m)
    _update(reads, writes, mk)
    return m


"""Kernel builders for the mk_sandwich_d1 module (4-launch design).

L1: head-sharded attention inner (ln0 -> in_proj -> conv -> SSD -> gate ->
    rms -> norm_w), 4 heads/core over the full sequence. Output ynF bf16
    [512 x seq] per core.

All programs are identical across cores; per-core differences live in the
host-sliced inputs (heads are LOCAL 0..3 everywhere).
"""
import contextlib
import numpy as np
import concourse.bass as bass
import concourse.mybir as mybir

F32 = mybir.dt.float32
BF16 = mybir.dt.bfloat16
U32 = mybir.dt.uint32
AF = mybir.ActivationFunctionType
ALU = mybir.AluOpType
AX = mybir.AxisListType.X

NC = 8
D_MODEL = 2048
D_INNER = 4096
NHEADS = 32
HL = 4
HEADDIM = 128
DSTATE = 64
CHUNK = 128
BLK = 512
CPB = BLK // CHUNK
KD = D_MODEL // 128
NFT = 10
FW_CORE = NFT * 128

LN_EPS = 1e-5
RMS_EPS = 1e-6




def build_l1(seq=8192):
    nb = seq // BLK
    nc = bass.Bass("TRN2", target_bir_lowering=False, debug=False)
    fw = Fw(nc)

    uT = nc.dram_tensor("uT", [D_MODEL, seq], F32, kind="ExternalInput").ap()
    wT = nc.dram_tensor("wT", [D_MODEL, FW_CORE], F32, kind="ExternalInput").ap()
    convw = nc.dram_tensor("convw", [128, 5, 4], F32, kind="ExternalInput").ap()
    convb = nc.dram_tensor("convb", [128, 5], F32, kind="ExternalInput").ap()
    alog = nc.dram_tensor("alog", [32, 1], F32, kind="ExternalInput").ap()
    dtb = nc.dram_tensor("dtb", [32, 1], F32, kind="ExternalInput").ap()
    dskr = nc.dram_tensor("dskr", [1, HL], F32, kind="ExternalInput").ap()
    nwc = nc.dram_tensor("nwc", [128, HL], F32, kind="ExternalInput").ap()

    ynF = nc.dram_tensor("ynF", [HL * 128, seq], BF16, kind="ExternalOutput").ap()

    es = contextlib.ExitStack()
    bufs = {}

    def sb(name, shape, dt=F32):
        t = es.enter_context(nc.sbuf_tensor(name, shape, dt))
        bufs[name] = Buf(name)
        return t

    def pst(name, shape, dt=F32):
        t = es.enter_context(nc.psum_tensor(name, shape, dt))
        bufs[name] = Buf(name)
        return t

    def B(name):
        if name not in bufs:
            bufs[name] = Buf(name)
        return bufs[name]

    NB2 = 4

    wbf = sb("wbf", [128, KD, FW_CORE], BF16)
    ubf = sb("ubf", [128, 2, KD, BLK], BF16)
    zx = sb("zx", [128, 7, BLK])
    xcbf = sb("xcbf", [128, 2, HL, BLK], BF16)
    bcbf = sb("bcbf", [128, 2, BLK], BF16)
    carry = sb("carry", [128, 5, 4])
    wk = sb("wk", [128, 2, BLK + 4])
    cvtmp = sb("cvtmp", [128, 5, BLK])
    sgt = sb("sgt", [128, 2, BLK], BF16)
    convw_sb = sb("convw_sb", [128, 5, 4])
    convb_sb = sb("convb_sb", [128, 5])
    alog_sb = sb("alog_sb", [32, 1])
    acol = sb("acol", [32, 1])
    dtb_sb = sb("dtb_sb", [32, 1])
    dskr_sb = sb("dskr_sb", [1, HL])
    nwc_sb = sb("nwc_sb", [128, HL])
    dcol4 = sb("dcol4", [128, HL])
    dI = sb("dI", [128, HL, 128], BF16)
    ones_col_bf = sb("ones_col_bf", [128, 1], BF16)
    ones1_f = sb("ones1_f", [1, 128])
    ones1_bf = sb("ones1_bf", [1, 128], BF16)
    row_bf = sb("row_bf", [1, 2, BLK], BF16)
    ones32 = sb("ones32", [32, CHUNK])
    i_f32 = sb("i_f32", [128, 128])
    i_bf = sb("i_bf", [128, 128], BF16)
    tri01 = sb("tri01", [128, 128])
    wones = sb("wones", [128, NFT])
    epsc = sb("epsc", [128, 3])
    sel4 = sb("sel4", [4, HL, 128])
    mean_r = sb("mean_r", [1, BLK])
    var_r = sb("var_r", [1, BLK])
    rstd_r = sb("rstd_r", [1, BLK])
    negmr_r = sb("negmr_r", [1, BLK])
    bc_r = sb("bc_r", [128, BLK])
    bc_negmr = sb("bc_negmr", [128, BLK])
    evtmp = sb("evtmp", [128, 2, BLK])
    sqbf = sb("sqbf", [128, 2, BLK], BF16)
    dtv = sb("dtv", [32, BLK])
    sp1 = sb("sp1", [32, BLK])
    da_f = sb("da_f", [32, BLK])
    a_f = sb("a_f", [32, BLK])
    dcy = sb("dcy", [32, 2, CHUNK])
    dtd = sb("dtd", [32, 2, CHUNK])
    dtvT = sb("dtvT", [128, 2, CPB, 32])
    dtdT = sb("dtdT", [128, 2, CPB, 32])
    aT = sb("aT", [128, 2, CPB, 32])
    expal = sb("expal", [128, 2, CPB, HL])
    al_row = sb("al_row", [1, 2, 32])
    ea_row = sb("ea_row", [1, 2, 32])
    scores_m = sb("scores_m", [128, CPB, 128])
    bt_sb = sb("bt_sb", [128, CPB, 128], BF16)
    a_row4 = sb("a_row4", [4, CPB, 128])
    cch = sb("cch", [64, CPB, 128], BF16)
    xt_sb = sb("xt_sb", [128, 16, 128], BF16)
    rbuf = sb("rbuf", [128, NB2, 128])
    lbuf = sb("lbuf", [128, NB2, 128])
    smdt = sb("smdt", [128, NB2, 128])
    w_bf = sb("w_bf", [128, 16, 128], BF16)
    bw_bf = sb("bw_bf", [128, 16, 64], BF16)
    expa_e = sb("expa_e", [64, NB2, 128])
    cexp = sb("cexp", [64, 16, 128], BF16)
    s_run = sb("s_run", [64, HL, 128])
    s_bf = sb("s_bf", [64, HL, 128], BF16)
    sigz = sb("sigz", [128, 2, HL, BLK], BF16)
    ygh = sb("ygh", [128, HL, BLK], BF16)
    ssq_r = sb("ssq_r", [1, 2, BLK])
    yrow = sb("yrow", [1, 2, BLK], BF16)
    ysq = sb("ysq", [128, 2, BLK], BF16)
    yn_bf = sb("yn_bf", [128, HL, BLK], BF16)

    psA = pst("psA", [128, BLK])
    psB = pst("psB", [128, BLK])
    psP = pst("psP", [128, BLK])
    psQ = pst("psQ", [128, BLK])
    psU = pst("psU", [128, BLK])
    psV = pst("psV", [128, BLK])
    psG = pst("psG", [128, 128], BF16)
    psH = pst("psH", [128, 128], BF16)
    psPQ = [psP, psQ]
    psUV = [psU, psV]
    psGH = [psG, psH]
    psAB = [psA, psB]

    hd, hd2, sd = fw.hd, fw.hd2, fw.sd
    pe, act, dve, pool = fw.pe, fw.act, fw.dve, fw.pool

    ld = fw.stream("ld")
    lw = fw.stream("lw")
    lwc = fw.stream("lwc")
    so = fw.stream("so")

    ED(lwc, sd, wbf[:], wT.rearrange("(kc p) f -> p kc f", p=128), writes=[B("wbf")])
    ED(lw, hd, convw_sb[:], convw, writes=[B("convw_sb")])
    ED(lw, hd, convb_sb[:], convb, writes=[B("convb_sb")])
    ED(lw, hd, alog_sb[:], alog, writes=[B("alog_sb")])
    ED(lw, hd, dtb_sb[:], dtb, writes=[B("dtb_sb")])
    ED(lw, hd, dskr_sb[:], dskr, writes=[B("dskr_sb")])
    ED(lw, hd, nwc_sb[:], nwc, writes=[B("nwc_sb")])

    E(pool, "memset", i_f32[:], 1.0, writes=[B("i_f32")])
    E(pool, "affine_select", i_f32[:], i_f32[:], pattern=[[-1, 128]],
      compare_op=ALU.is_equal, fill=0.0, base=0, channel_multiplier=1,
      reads=[B("i_f32")], writes=[B("i_f32")])
    E(pool, "memset", tri01[:], 1.0, writes=[B("tri01")])
    E(pool, "affine_select", tri01[:], tri01[:], pattern=[[1, 128]],
      compare_op=ALU.is_ge, fill=0.0, base=0, channel_multiplier=-1,
      reads=[B("tri01")], writes=[B("tri01")])
    E(dve, "tensor_copy", i_bf[:], i_f32[:], reads=[B("i_f32")], writes=[B("i_bf")])
    E(dve, "memset", ones_col_bf[:], 1.0, writes=[B("ones_col_bf")])
    E(dve, "memset", ones1_f[:], 1.0, writes=[B("ones1_f")])
    E(dve, "memset", ones1_bf[:], 1.0, writes=[B("ones1_bf")])
    E(dve, "memset", ones32[:], 1.0, writes=[B("ones32")])
    E(dve, "memset", carry[:], 0.0, writes=[B("carry")])
    for h in range(HL):
        E(dve, "memset", s_run[:, h, :], 0.0, writes=[B(f"s_run{h}")])
    E(dve, "memset", epsc[:, 0:1], LN_EPS, writes=[B("epsc")])
    E(dve, "memset", epsc[:, 1:2], RMS_EPS, writes=[B("epsc")])
    E(dve, "memset", epsc[:, 2:3], 1e-30, writes=[B("epsc")])
    E(pool, "memset", sel4[:], 1.0, writes=[B("sel4")])
    E(pool, "affine_select", sel4[:], sel4[:], pattern=[[-1, HL], [0, 128]],
      compare_op=ALU.is_equal, fill=0.0, base=0, channel_multiplier=1,
      reads=[B("sel4")], writes=[B("sel4")])

    E(act, "activation", acol[:], alog_sb[:], AF.Exp,
      reads=[B("alog_sb")], writes=[B("acol")])
    E(dve, "tensor_scalar_mul", acol[:], acol[:], -1.0,
      reads=[B("acol")], writes=[B("acol")])

    E(pe, "matmul", psU[:, 0:HL], ones1_f[:], dskr_sb[:], start=True, stop=True,
      reads=[B("ones1_f"), B("dskr_sb")], writes=[B("psU")])
    E(dve, "tensor_copy", dcol4[:], psU[:, 0:HL], reads=[B("psU")], writes=[B("dcol4")])
    for h in range(HL):
        E(dve, "tensor_scalar", dI[:, h, :], i_f32[:], dcol4[:, h:h + 1], None,
          ALU.mult, reads=[B("i_f32"), B("dcol4")], writes=[B("dI")])

    for ft in range(NFT):
        for k in range(KD):
            E(pe, "matmul", psP[0:1, 0:128], ones_col_bf[:], wbf[:, k, ft * 128:(ft + 1) * 128],
              start=(k == 0), stop=(k == KD - 1),
              reads=[B("ones_col_bf"), B("wbf")], writes=[B("psP")])
        E(dve, "tensor_copy", mean_r[:, 0:128], psP[0:1, 0:128],
          reads=[B("psP")], writes=[B("mean_r")])
        E(pe, "transpose", psQ[:, 0:1], mean_r[0:1, 0:128], i_f32[0:1, 0:1],
          reads=[B("mean_r"), B("i_f32")], writes=[B("psQ")])
        E(dve, "tensor_copy", wones[:, ft:ft + 1], psQ[:, 0:1],
          reads=[B("psQ")], writes=[B("wones")])

    def gen_X(b):
        t0 = b * BLK
        bp = b % 2
        if b == 0:
            for k in range(KD):
                ED(ld, sd, ubf[:, 0, k, :], uT[k * 128:(k + 1) * 128, 0:BLK],
                   writes=[B("ubf0")], new_group=(k == 0))
        if b + 1 < nb:
            pp = (b + 1) % 2
            t0p = (b + 1) * BLK
            for k in range(KD):
                ED(ld, sd, ubf[:, pp, k, :], uT[k * 128:(k + 1) * 128, t0p:t0p + BLK],
                   writes=[B(f"ubf{pp}")], new_group=(k == 0))
        for k in range(KD):
            sp = k % 2
            if sp == 0:
                E(pool, "tensor_mul", sqbf[:, sp, :], ubf[:, bp, k, :], ubf[:, bp, k, :],
                  reads=[B(f"ubf{bp}")], writes=[B(f"sqbf{sp}")])
            else:
                E(act, "activation", sqbf[:, sp, :], ubf[:, bp, k, :], AF.Square,
                  reads=[B(f"ubf{bp}")], writes=[B(f"sqbf{sp}")])
            E(pe, "matmul", psA[0:1, :], ones_col_bf[:], ubf[:, bp, k, :],
              start=(k == 0), stop=(k == KD - 1),
              reads=[B("ones_col_bf"), B(f"ubf{bp}")], writes=[B("psA")])
            E(pe, "matmul", psB[0:1, :], ones_col_bf[:], sqbf[:, sp, :],
              start=(k == 0), stop=(k == KD - 1),
              reads=[B("ones_col_bf"), B(f"sqbf{sp}")], writes=[B("psB")])
        E(dve, "tensor_scalar_mul", mean_r[:], psA[0:1, :], 1.0 / D_MODEL,
          reads=[B("psA")], writes=[B("mean_r")])
        E(dve, "tensor_scalar_mul", var_r[:], psB[0:1, :], 1.0 / D_MODEL,
          reads=[B("psB")], writes=[B("var_r")])
        E(dve, "tensor_mul", negmr_r[:], mean_r[:], mean_r[:],
          reads=[B("mean_r")], writes=[B("negmr_r")])
        E(dve, "tensor_sub", var_r[:], var_r[:], negmr_r[:],
          reads=[B("var_r"), B("negmr_r")], writes=[B("var_r")])
        E(act, "activation", rstd_r[:], var_r[:], AF.Ln, bias=epsc[0:1, 0:1],
          reads=[B("var_r"), B("epsc")], writes=[B("rstd_r")])
        E(act, "activation", row_bf[:, 0, :], rstd_r[:], AF.Exp, scale=-0.5,
          reads=[B("rstd_r")], writes=[B("row_bf0")])
        yield

        E(dve, "tensor_mul", negmr_r[:], mean_r[:], row_bf[:, 0, :],
          reads=[B("mean_r"), B("row_bf0")], writes=[B("negmr_r")])
        E(dve, "tensor_scalar_mul", row_bf[:, 1, :], negmr_r[:], -1.0,
          reads=[B("negmr_r")], writes=[B("row_bf1")])
        E(pe, "matmul", psA[:], ones1_bf[:], row_bf[0:1, 0, :], start=True, stop=True,
          reads=[B("ones1_bf"), B("row_bf0")], writes=[B("psA")])
        E(dve, "tensor_copy", bc_r[:], psA[:], reads=[B("psA")], writes=[B("bc_r")])
        E(pe, "matmul", psB[:], ones1_bf[:], row_bf[0:1, 1, :], start=True, stop=True,
          reads=[B("ones1_bf"), B("row_bf1")], writes=[B("psB")])
        E(dve, "tensor_copy", bc_negmr[:], psB[:], reads=[B("psB")], writes=[B("bc_negmr")])
        yield

        ft_order = [9, 4, 5, 6, 7, 8, 0, 1, 2, 3]
        for fi, ft in enumerate(ft_order):
            zs = ft if ft < 4 else (4 if ft == 9 else 5 + ((ft - 4) % 2))
            ps, psb = (psA, "psA") if fi % 2 == 0 else (psB, "psB")
            sp = fi % 2
            for k in range(KD):
                E(pe, "matmul", ps[:], wbf[:, k, ft * 128:(ft + 1) * 128], ubf[:, bp, k, :],
                  start=(k == 0), stop=(k == KD - 1),
                  reads=[B("wbf"), B(f"ubf{bp}")], writes=[B(psb)])
                if k == 7:
                    yield
            E(dve, "tensor_mul", evtmp[:, sp, :], ps[:], bc_r[:],
              reads=[B(psb), B("bc_r")], writes=[B(f"evtmp{sp}")])
            E(dve, "scalar_tensor_tensor", zx[:, zs, :], bc_negmr[:],
              wones[:, ft:ft + 1], evtmp[:, sp, :], ALU.mult, ALU.add,
              reads=[B("bc_negmr"), B("wones"), B(f"evtmp{sp}")], writes=[B(f"zxs{zs}")])

            if 4 <= ft <= 8:
                ci = ft - 4
                wp = ci % 2
                E(pool, "tensor_copy", wk[:, wp, 0:4], carry[:, ci, :],
                  reads=[B("carry")], writes=[B(f"wk{wp}")])
                E(pool, "tensor_copy", wk[:, wp, 4:4 + BLK], zx[:, zs, :],
                  reads=[B(f"zxs{zs}")], writes=[B(f"wk{wp}")])
                E(pool, "tensor_copy", carry[:, ci, :], wk[:, wp, BLK:BLK + 4],
                  reads=[B(f"wk{wp}")], writes=[B("carry")])
                E(dve, "tensor_scalar", cvtmp[:, ci, :], wk[:, wp, 1:1 + BLK],
                  convw_sb[:, ci, 0:1], convb_sb[:, ci:ci + 1], ALU.mult, ALU.add,
                  reads=[B(f"wk{wp}"), B("convw_sb"), B("convb_sb")], writes=[B(f"cvtmp{ci}")])
                for tap in range(1, 4):
                    E(dve, "scalar_tensor_tensor", cvtmp[:, ci, :],
                      wk[:, wp, 1 + tap:1 + tap + BLK],
                      convw_sb[:, ci, tap:tap + 1], cvtmp[:, ci, :], ALU.mult, ALU.add,
                      reads=[B(f"wk{wp}"), B("convw_sb"), B(f"cvtmp{ci}")],
                      writes=[B(f"cvtmp{ci}")])

            if ft == 9:
                E(act, "activation", sp1[:], zx[0:32, 4, :], AF.Abs, bias=dtb_sb[:],
                  reads=[B("zxs4"), B("dtb_sb")], writes=[B("sp1")])
                E(act, "activation", sp1[:], sp1[:], AF.Exp, scale=-1.0,
                  reads=[B("sp1")], writes=[B("sp1")])
                E(act, "activation", sp1[:], sp1[:], AF.Ln, bias=1.0,
                  reads=[B("sp1")], writes=[B("sp1")])
                E(act, "activation", dtv[:], zx[0:32, 4, :], AF.Relu, bias=dtb_sb[:],
                  reads=[B("zxs4"), B("dtb_sb")], writes=[B("dtv")])
                E(dve, "tensor_add", dtv[:], dtv[:], sp1[:],
                  reads=[B("dtv"), B("sp1")], writes=[B("dtv")])
                E(dve, "tensor_scalar", da_f[:], dtv[:], acol[:], None, ALU.mult,
                  reads=[B("dtv"), B("acol")], writes=[B("da_f")])
                for c in range(CPB):
                    E(dve, "tensor_tensor_scan", a_f[:, c * CHUNK:(c + 1) * CHUNK],
                      ones32[:], da_f[:, c * CHUNK:(c + 1) * CHUNK], 0.0, ALU.mult, ALU.add,
                      reads=[B("ones32"), B("da_f")], writes=[B("a_f")])
                for c in range(CPB):
                    cs = slice(c * CHUNK, (c + 1) * CHUNK)
                    cp = c % 2
                    alast = a_f[:, c * CHUNK + CHUNK - 1:c * CHUNK + CHUNK]
                    E(act, "activation", dcy[:, cp, :], a_f[:, cs], AF.Exp, scale=-1.0,
                      bias=alast, reads=[B("a_f")], writes=[B(f"dcy{cp}")])
                    E(dve, "tensor_mul", dtd[:, cp, :], dtv[:, cs], dcy[:, cp, :],
                      reads=[B("dtv"), B(f"dcy{cp}")], writes=[B(f"dtd{cp}")])
            yield

            if ft == 9:
                rot = 1
                for c in range(CPB):
                    cs = slice(c * CHUNK, (c + 1) * CHUNK)
                    cp = c % 2
                    alast = a_f[:, c * CHUNK + CHUNK - 1:c * CHUNK + CHUNK]
                    for src, srcb, dstt, dstb2 in ((dtv[:, cs], "dtv", dtvT, "dtvT"),
                                                   (dtd[:, cp, :], f"dtd{cp}", dtdT, "dtdT"),
                                                   (a_f[:, cs], "a_f", aT, "aT")):
                        ps, psb = psAB[rot % 2], ("psA", "psB")[rot % 2]
                        rot += 1
                        E(pe, "transpose", ps[:, 0:32], src, i_f32[0:32, 0:32],
                          reads=[B(srcb), B("i_f32")], writes=[B(psb)])
                        E(dve, "tensor_copy", dstt[:, bp, c, :], ps[:, 0:32],
                          reads=[B(psb)], writes=[B(f"{dstb2}{bp}")])
                    ps, psb = psAB[rot % 2], ("psA", "psB")[rot % 2]
                    rot += 1
                    E(pe, "transpose", ps[0:1, 0:32], alast, i_f32[0:32, 0:32],
                      reads=[B("a_f"), B("i_f32")], writes=[B(psb)])
                    E(dve, "tensor_copy", al_row[:, cp, :], ps[0:1, 0:32],
                      reads=[B(psb)], writes=[B(f"al_row{cp}")])
                    E(act, "activation", ea_row[:, cp, :], al_row[:, cp, :], AF.Exp,
                      reads=[B(f"al_row{cp}")], writes=[B(f"ea_row{cp}")])
                    ps, psb = psAB[rot % 2], ("psA", "psB")[rot % 2]
                    rot += 1
                    E(pe, "matmul", ps[:, 0:HL], ones1_f[:], ea_row[0:1, cp, 0:HL],
                      start=True, stop=True,
                      reads=[B("ones1_f"), B(f"ea_row{cp}")], writes=[B(psb)])
                    E(dve, "tensor_copy", expal[:, bp, c, :], ps[:, 0:HL],
                      reads=[B(psb)], writes=[B(f"expal{bp}")])
                    yield

        for ci in range(5):
            sp = ci % 2
            E(act, "activation", sgt[:, sp, :], cvtmp[:, ci, :], AF.Sigmoid,
              reads=[B(f"cvtmp{ci}")], writes=[B(f"sgt{sp}")])
            dst = xcbf[:, bp, ci, :] if ci < 4 else bcbf[:, bp, :]
            dstb = f"xcbf{bp}{ci}" if ci < 4 else f"bcbf{bp}"
            E(pool, "tensor_mul", dst, cvtmp[:, ci, :], sgt[:, sp, :],
              reads=[B(f"cvtmp{ci}"), B(f"sgt{sp}")], writes=[B(dstb)])
        for ft in range(4):
            E(act, "activation", sigz[:, bp, ft, :], zx[:, ft, :], AF.Sigmoid,
              reads=[B(f"zxs{ft}")], writes=[B(f"sigz{bp}{ft}")])
        yield

    def gen_Y(b):
        t0 = b * BLK
        bp = b % 2
        rotg = 0

        def nextg():
            nonlocal rotg
            sl = rotg % 2
            rotg += 1
            return (psGH[sl][:], ("psG", "psH")[sl])

        for c in range(CPB):
            cs = slice(c * CHUNK, (c + 1) * CHUNK)
            E(act, "activation", cch[:, c, :], bcbf[64:128, bp, cs], AF.Copy,
              reads=[B(f"bcbf{bp}")], writes=[B(f"cch{c}")])
        rot = 0
        for c in range(CPB):
            cs = slice(c * CHUNK, (c + 1) * CHUNK)
            ps, psb = psPQ[rot % 2], ("psP", "psQ")[rot % 2]
            rot += 1
            E(pe, "matmul", ps[:, 0:128], bcbf[0:64, bp, cs], cch[:, c, :],
              start=True, stop=True, reads=[B(f"bcbf{bp}"), B(f"cch{c}")], writes=[B(psb)])
            E(dve, "tensor_mul", scores_m[:, c, :], ps[:, 0:128], tri01[:],
              reads=[B(psb), B("tri01")], writes=[B(f"scores{c}")])
            pga, pgab = nextg()
            E(pe, "transpose", pga, bcbf[:, bp, cs], i_bf[:],
              reads=[B(f"bcbf{bp}"), B("i_bf")], writes=[B(pgab)])
            E(act, "activation", bt_sb[:, c, :], pga, AF.Copy,
              reads=[B(pgab)], writes=[B(f"bt{c}")])
            ps, psb = psPQ[rot % 2], ("psP", "psQ")[rot % 2]
            rot += 1
            E(pe, "transpose", ps[0:4, 0:128], aT[:, bp, c, 0:HL], i_f32[:],
              reads=[B(f"aT{bp}"), B("i_f32")], writes=[B(psb)])
            E(act, "activation", a_row4[:, c, :], ps[0:4, 0:128], AF.Copy,
              reads=[B(psb)], writes=[B(f"a_row4{c}")])
            yield

        for u in range(16):
            c, h = u // HL, u % HL
            cs = slice(c * CHUNK, (c + 1) * CHUNK)
            s = u % NB2
            pga, pgab = nextg()
            E(pe, "transpose", pga, xcbf[:, bp, h, cs], i_bf[:],
              reads=[B(f"xcbf{bp}{h}"), B("i_bf")], writes=[B(pgab)])
            E(act, "activation", xt_sb[:, u, :], pga, AF.Copy,
              reads=[B(pgab)], writes=[B(f"xt_sb{u}")])
            pu, pub = psUV[u % 2], ("psU", "psV")[u % 2]
            E(pe, "matmul", pu[:, 0:128], sel4[:, h, :], a_row4[:, c, :],
              start=True, stop=True,
              reads=[B("sel4"), B(f"a_row4{c}")], writes=[B(pub)])
            E(act, "activation", rbuf[:, s, :], pu[:, 0:128], AF.Relu,
              scale=-1.0, bias=aT[:, bp, c, h:h + 1],
              reads=[B(pub), B(f"aT{bp}")], writes=[B(f"rbuf{s}")])
            E(act, "activation", expa_e[:, s, :], pu[0:64, 0:128], AF.Exp,
              reads=[B(pub)], writes=[B(f"expa_e{s}")])
            E(act, "activation", lbuf[:, s, :], rbuf[:, s, :], AF.Exp, scale=-1.0,
              reads=[B(f"rbuf{s}")], writes=[B(f"lbuf{s}")])
            E(dve, "tensor_scalar", smdt[:, s, :], scores_m[:, c, :],
              dtvT[:, bp, c, h:h + 1], None, ALU.mult,
              reads=[B(f"scores{c}"), B(f"dtvT{bp}")], writes=[B(f"smdt{s}")])
            E(pool, "tensor_mul", w_bf[:, u, :], smdt[:, s, :], lbuf[:, s, :],
              reads=[B(f"smdt{s}"), B(f"lbuf{s}")], writes=[B(f"w_bf{u}")])
            E(dve, "tensor_scalar", bw_bf[:, u, :], bt_sb[:, c, 0:64],
              dtdT[:, bp, c, h:h + 1], None, ALU.mult,
              reads=[B(f"bt{c}"), B(f"dtdT{bp}")], writes=[B(f"bw_bf{u}")])
            E(pool, "tensor_mul", cexp[:, u, :], cch[:, c, :], expa_e[:, s, :],
              reads=[B(f"cch{c}"), B(f"expa_e{s}")], writes=[B(f"cexp{u}")])
            yield

        for h in range(HL):
            E(dve, "tensor_copy", s_bf[:, h, :], s_run[:, h, :],
              reads=[B(f"s_run{h}")], writes=[B(f"s_bf{h}")])
        for c in range(CPB):
            cs = slice(c * CHUNK, (c + 1) * CHUNK)
            for h in range(HL):
                u = c * HL + h
                pu, pub = psUV[u % 2], ("psU", "psV")[u % 2]
                ps, psb = psPQ[u % 2], ("psP", "psQ")[u % 2]
                E(pe, "matmul", pu[:, 0:128], xt_sb[:, u, :], w_bf[:, u, :],
                  start=True, stop=False,
                  reads=[B(f"xt_sb{u}"), B(f"w_bf{u}")], writes=[B(pub)])
                E(pe, "matmul", pu[:, 0:128], xt_sb[:, u, :], dI[:, h, :],
                  start=False, stop=False,
                  reads=[B(f"xt_sb{u}"), B("dI")], writes=[B(pub)])
                E(pe, "matmul", pu[:, 0:128], s_bf[:, h, :], cexp[:, u, :],
                  start=False, stop=True,
                  reads=[B(f"s_bf{h}"), B(f"cexp{u}")], writes=[B(pub)])
                E(pe, "matmul", ps[0:64, 0:128], bw_bf[:, u, :], xt_sb[:, u, :],
                  start=True, stop=True,
                  reads=[B(f"bw_bf{u}"), B(f"xt_sb{u}")], writes=[B(psb)])
                E(dve, "scalar_tensor_tensor", s_run[:, h, :], s_run[:, h, :],
                  expal[0:64, bp, c, h:h + 1], ps[0:64, 0:128], ALU.mult, ALU.add,
                  reads=[B(f"s_run{h}"), B(f"expal{bp}"), B(psb)], writes=[B(f"s_run{h}")])
                if c < CPB - 1:
                    E(dve, "tensor_copy", s_bf[:, h, :], s_run[:, h, :],
                      reads=[B(f"s_run{h}")], writes=[B(f"s_bf{h}")])
                E(dve, "tensor_mul", ygh[:, h, cs], pu[:, 0:128], sigz[:, bp, h, cs],
                  reads=[B(pub), B(f"sigz{bp}{h}")], writes=[B(f"ygh{h}")])
            if c < CPB - 1:
                yield

    def gen_R(b):
        t0 = b * BLK
        for h in range(HL):
            sp = h % 2
            E(act, "activation", ysq[:, sp, :], ygh[:, h, :], AF.Square,
              reads=[B(f"ygh{h}")], writes=[B(f"ysq{sp}")])
            ps2, psb2 = psPQ[h % 2], ("psP", "psQ")[h % 2]
            E(pe, "matmul", ps2[0:1, :], ones_col_bf[:], ysq[:, sp, :],
              start=True, stop=True,
              reads=[B("ones_col_bf"), B(f"ysq{sp}")], writes=[B(psb2)])
            E(act, "activation", ssq_r[:, sp, :], ps2[0:1, :], AF.Ln,
              scale=1.0 / HEADDIM, bias=epsc[0:1, 1:2],
              reads=[B(psb2), B("epsc")], writes=[B(f"ssq{sp}")])
            E(act, "activation", yrow[:, sp, :], ssq_r[:, sp, :], AF.Exp,
              scale=-0.5,
              reads=[B(f"ssq{sp}")], writes=[B(f"yrow{sp}")])
            pu2, pub2 = psUV[h % 2], ("psU", "psV")[h % 2]
            E(pe, "matmul", pu2[:], ones1_bf[:], yrow[0:1, sp, :],
              start=True, stop=True,
              reads=[B("ones1_bf"), B(f"yrow{sp}")], writes=[B(pub2)])
            E(dve, "scalar_tensor_tensor", yn_bf[:, h, :], ygh[:, h, :],
              nwc_sb[:, h:h + 1], pu2[:], ALU.mult, ALU.mult,
              reads=[B(f"ygh{h}"), B("nwc_sb"), B(pub2)], writes=[B(f"yn_bf{h}")])
            ED(so, hd2, ynF[h * 128:(h + 1) * 128, t0:t0 + BLK], yn_bf[:, h, :],
               reads=[B(f"yn_bf{h}")])
            yield

    for _ in gen_X(0):
        pass
    gr = None
    for b in range(nb):
        gy = gen_Y(b)
        gx = gen_X(b + 1) if b + 1 < nb else None
        done_y = False
        done_x = gx is None
        while not (done_y and done_x):
            if gr is not None:
                try:
                    next(gr)
                except StopIteration:
                    gr = None
            if not done_y:
                try:
                    next(gy)
                except StopIteration:
                    done_y = True
            if not done_x:
                try:
                    next(gx)
                except StopIteration:
                    done_x = True
        if gr is not None:
            for _ in gr:
                pass
        gr = gen_R(b)
    for _ in gr:
        pass

    fw.pool.wait_only(wait=[so.now()])
    fw.finalize(es)
    es.close()
    return nc


def build_l2():
    TOK = 1024
    KI = 32
    NSH = 22
    nc = bass.Bass("TRN2", target_bir_lowering=False, debug=False)
    fw = Fw(nc)

    ynF_d = nc.dram_tensor("ynF_sl", [D_INNER, TOK], BF16, kind="ExternalInput").ap()
    opT = nc.dram_tensor("opT", [KD, 2, 128, KI // 2, 128], F32, kind="ExternalInput").ap()
    uT_sl = nc.dram_tensor("uT_sl", [D_MODEL, TOK], F32, kind="ExternalInput").ap()
    gwT = nc.dram_tensor("gwT", [D_MODEL, 16], F32, kind="ExternalInput").ap()
    swg = nc.dram_tensor("swg", [NSH, 128, KD, 128], F32, kind="ExternalInput").ap()
    swu = nc.dram_tensor("swu", [NSH, 128, KD, 128], F32, kind="ExternalInput").ap()
    swd = nc.dram_tensor("swd", [KD, 128, NSH, 128], F32, kind="ExternalInput").ap()

    x1F = nc.dram_tensor("x1F", [D_MODEL, TOK], F32, kind="ExternalOutput").ap()
    ln2T = nc.dram_tensor("ln2T", [TOK, D_MODEL], BF16, kind="ExternalOutput").ap()
    topk = nc.dram_tensor("topk", [128, 8, 8], F32, kind="ExternalOutput").ap()
    sharedF = nc.dram_tensor("sharedF", [D_MODEL, TOK], F32, kind="ExternalOutput").ap()

    es = contextlib.ExitStack()
    bufs = {}

    def sb(name, shape, dt=F32):
        t = es.enter_context(nc.sbuf_tensor(name, shape, dt))
        bufs[name] = Buf(name)
        return t

    def pst(name, shape, dt=F32):
        t = es.enter_context(nc.psum_tensor(name, shape, dt))
        bufs[name] = Buf(name)
        return t

    def B(name):
        if name not in bufs:
            bufs[name] = Buf(name)
        return bufs[name]

    ynb = sb("ynb", [128, KI, TOK], BF16)
    sF = ynb
    zzb = sb("zzb", [128, KD, TOK], BF16)
    ln2b = sb("ln2b", [128, KD, TOK], BF16)
    wop = sb("wop", [128, 2, KI // 2, 128], BF16)
    swgu_t = sb("swgu_t", [128, 4, KD, 128], BF16)
    swd_t = sb("swd_t", [128, 2, NSH, 128], BF16)
    gw_sb = sb("gw_sb", [128, KD, 16])
    u_t = sb("u_t", [128, 2, TOK])
    ones_col_bf = sb("ones_col_bf", [128, 1], BF16)
    ones_col_f = sb("ones_col_f", [128, 1])
    ones1_f = sb("ones1_f", [1, 128])
    i_f32 = sb("i_f32", [128, 128])
    i_bf = sb("i_bf", [128, 128], BF16)
    epsc = sb("epsc", [128, 2])
    sqbf = sb("sqbf", [128, 512], BF16)
    trbf = sb("trbf", [128, 8, 128], BF16)
    scrow = sb("scrow", [16, 128])
    mean_r = sb("mean_r", [1, TOK])
    var_r = sb("var_r", [1, TOK])
    rstd_r = sb("rstd_r", [1, TOK])
    bc_m = sb("bc_m", [128, TOK])
    bc_r = sb("bc_r", [128, TOK])
    evtmp = sb("evtmp", [128, 2, TOK])
    gactf = sb("gactf", [128, 512])
    scT = sb("scT", [128, 16])
    prob = sb("prob", [128, 16])
    mcol = sb("mcol", [128, 1])
    mx8 = sb("mx8", [128, 8])
    ix8 = sb("ix8", [128, 8], U32)
    w4 = sb("w4", [128, 4])
    idx4f = sb("idx4f", [128, 4])
    tk_sb = sb("tk_sb", [128, 8, 8])

    psA = pst("psA", [128, 512])
    psB = pst("psB", [128, 512])
    psC = pst("psC", [128, 512])
    psD = pst("psD", [128, 512])
    psE = pst("psE", [128, 512])
    psF = pst("psF", [16, TOK])
    psG = pst("psG", [128, 128], BF16)

    hd, hd2, sd = fw.hd, fw.hd2, fw.sd
    pe, act, dve, pool = fw.pe, fw.act, fw.dve, fw.pool

    ldy = fw.stream("ldy")
    ldw = fw.stream("ldw")
    ldu = fw.stream("ldu")
    so = fw.stream("so")
    so2 = fw.stream("so2")

    ED(ldy, hd, ynb[:], ynF_d.rearrange("(kc p) t -> p kc t", p=128), writes=[B("ynb")])
    ED(ldw, sd, gw_sb[:], gwT.rearrange("(kc p) e -> p kc e", p=128), writes=[B("gw_sb")])
    E(pool, "memset", i_f32[:], 1.0, writes=[B("i_f32")])
    E(pool, "affine_select", i_f32[:], i_f32[:], pattern=[[-1, 128]],
      compare_op=ALU.is_equal, fill=0.0, base=0, channel_multiplier=1,
      reads=[B("i_f32")], writes=[B("i_f32")])
    E(dve, "tensor_copy", i_bf[:], i_f32[:], reads=[B("i_f32")], writes=[B("i_bf")])
    E(dve, "memset", ones_col_bf[:], 1.0, writes=[B("ones_col_bf")])
    E(dve, "memset", ones_col_f[:], 1.0, writes=[B("ones_col_f")])
    E(dve, "memset", ones1_f[:], 1.0, writes=[B("ones1_f")])
    E(dve, "memset", epsc[:, 0:1], LN_EPS, writes=[B("epsc")])
    E(dve, "memset", epsc[:, 1:2], 1e-20, writes=[B("epsc")])

    def stats_block(src_sb, src_buf, firstk, lastk, k):
        for b2 in range(2):
            ps, psb = (psA, "psA") if b2 == 0 else (psB, "psB")
            E(pe, "matmul", ps[0:1, :], ones_col_bf[:], src_sb[:, b2 * 512:(b2 + 1) * 512],
              start=firstk, stop=lastk,
              reads=[B("ones_col_bf"), B(src_buf)], writes=[B(psb)])
        for b2 in range(2):
            ps, psb = (psD, "psD") if b2 == 0 else (psE, "psE")
            E(act, "activation", sqbf[:], src_sb[:, b2 * 512:(b2 + 1) * 512] if True else None,
              AF.Square, reads=[B(src_buf)], writes=[B("sqbf")])
            E(pe, "matmul", ps[0:1, :], ones_col_bf[:], sqbf[:],
              start=firstk, stop=lastk,
              reads=[B("ones_col_bf"), B("sqbf")], writes=[B(psb)])

    def ln_rows():
        E(dve, "tensor_scalar_mul", mean_r[:, 0:512], psA[0:1, :], 1.0 / D_MODEL,
          reads=[B("psA")], writes=[B("mean_r")])
        E(dve, "tensor_scalar_mul", mean_r[:, 512:1024], psB[0:1, :], 1.0 / D_MODEL,
          reads=[B("psB")], writes=[B("mean_r")])
        E(dve, "tensor_scalar_mul", var_r[:, 0:512], psD[0:1, :], 1.0 / D_MODEL,
          reads=[B("psD")], writes=[B("var_r")])
        E(dve, "tensor_scalar_mul", var_r[:, 512:1024], psE[0:1, :], 1.0 / D_MODEL,
          reads=[B("psE")], writes=[B("var_r")])
        E(dve, "tensor_mul", bc_m[0:1, :], mean_r[:], mean_r[:],
          reads=[B("mean_r")], writes=[B("bc_m")])
        E(dve, "tensor_sub", var_r[:], var_r[:], bc_m[0:1, :],
          reads=[B("var_r"), B("bc_m")], writes=[B("var_r")])
        E(act, "activation", rstd_r[:], var_r[:], AF.Sqrt, bias=epsc[0:1, 0:1],
          reads=[B("var_r"), B("epsc")], writes=[B("rstd_r")])
        E(dve, "reciprocal", rstd_r[:], rstd_r[:], reads=[B("rstd_r")], writes=[B("rstd_r")])

    def bcast_rows(row, rowb, dst, dstb):
        for b2 in range(2):
            E(pe, "matmul", psC[:], ones1_f[:], row[0:1, b2 * 512:(b2 + 1) * 512],
              start=True, stop=True,
              reads=[B("ones1_f"), B(rowb)], writes=[B("psC")])
            E(dve, "tensor_copy", dst[:, b2 * 512:(b2 + 1) * 512], psC[:],
              reads=[B("psC")], writes=[B(dstb)])

    ED(ldw, sd, wop[:, 0, :, :], opT[0, 0], writes=[B("wop0")])
    for dd in range(KD):
        for half in range(2):
            k0 = half * (KI // 2)
            par = (dd * 2 + half) % 2
            if not (dd == KD - 1 and half == 1):
                ddn, halfn = (dd, 1) if half == 0 else (dd + 1, 0)
                ED(ldw, sd, wop[:, 1 - par, :, :], opT[ddn, halfn],
                   writes=[B(f"wop{1 - par}")])
            for b2 in range(2):
                ps, psb = (psA, "psA") if b2 == 0 else (psB, "psB")
                for k in range(KI // 2):
                    E(pe, "matmul", ps[:], wop[:, par, k, :],
                      ynb[:, k0 + k, b2 * 512:(b2 + 1) * 512],
                      start=(half == 0 and k == 0), stop=(half == 1 and k == KI // 2 - 1),
                      reads=[B(f"wop{par}"), B("ynb")], writes=[B(psb)])
        for b2 in range(2):
            ps, psb = (psA, "psA") if b2 == 0 else (psB, "psB")
            E(dve, "tensor_copy", zzb[:, dd, b2 * 512:(b2 + 1) * 512], ps[:],
              reads=[B(psb)], writes=[B("zzb")])

    for k in range(KD):
        stats_block(zzb[:, k, :], "zzb", k == 0, k == KD - 1, k)
    ln_rows()
    bcast_rows(mean_r, "mean_r", bc_m, "bc_m")
    bcast_rows(rstd_r, "rstd_r", bc_r, "bc_r")

    ED(ldu, hd, u_t[:, 0, :], uT_sl[0:128, :], writes=[B("ut0")])
    for k in range(KD):
        sp = k % 2
        if k + 1 < KD:
            ED(ldu, hd, u_t[:, 1 - sp, :], uT_sl[(k + 1) * 128:(k + 2) * 128, :],
               writes=[B(f"ut{1 - sp}")])
        E(dve, "tensor_sub", evtmp[:, sp, :], zzb[:, k, :], bc_m[:],
          reads=[B("zzb"), B("bc_m")], writes=[B(f"evtmp{sp}")])
        E(dve, "tensor_mul", evtmp[:, sp, :], evtmp[:, sp, :], bc_r[:],
          reads=[B(f"evtmp{sp}"), B("bc_r")], writes=[B(f"evtmp{sp}")])
        E(dve, "tensor_add", evtmp[:, sp, :], evtmp[:, sp, :], u_t[:, sp, :],
          reads=[B(f"evtmp{sp}"), B(f"ut{sp}")], writes=[B(f"evtmp{sp}")])
        ED(so, hd2, x1F[k * 128:(k + 1) * 128, :], evtmp[:, sp, :],
           reads=[B(f"evtmp{sp}")], writes=[B("x1Fd")])
        for b2 in range(2):
            ps, psb = (psA, "psA") if b2 == 0 else (psB, "psB")
            E(pe, "matmul", ps[0:1, :], ones_col_f[:],
              evtmp[:, sp, b2 * 512:(b2 + 1) * 512],
              start=(k == 0), stop=(k == KD - 1),
              reads=[B("ones_col_f"), B(f"evtmp{sp}")], writes=[B(psb)])
        for b2 in range(2):
            ps, psb = (psD, "psD") if b2 == 0 else (psE, "psE")
            E(act, "activation", sqbf[:], evtmp[:, sp, b2 * 512:(b2 + 1) * 512], AF.Square,
              reads=[B(f"evtmp{sp}")], writes=[B("sqbf")])
            E(pe, "matmul", ps[0:1, :], ones_col_bf[:], sqbf[:],
              start=(k == 0), stop=(k == KD - 1),
              reads=[B("ones_col_bf"), B("sqbf")], writes=[B(psb)])
    ln_rows()
    bcast_rows(mean_r, "mean_r", bc_m, "bc_m")
    bcast_rows(rstd_r, "rstd_r", bc_r, "bc_r")

    ED(ldu, hd, u_t[:, 0, :], x1F[0:128, :], reads=[B("x1Fd")], writes=[B("ut0")])
    for k in range(KD):
        sp = k % 2
        if k + 1 < KD:
            ED(ldu, hd, u_t[:, 1 - sp, :], x1F[(k + 1) * 128:(k + 2) * 128, :],
               reads=[B("x1Fd")], writes=[B(f"ut{1 - sp}")])
        E(dve, "tensor_sub", evtmp[:, sp, :], u_t[:, sp, :], bc_m[:],
          reads=[B(f"ut{sp}"), B("bc_m")], writes=[B(f"evtmp{sp}")])
        E(dve, "tensor_mul", evtmp[:, sp, :], evtmp[:, sp, :], bc_r[:],
          reads=[B(f"evtmp{sp}"), B("bc_r")], writes=[B(f"evtmp{sp}")])
        E(dve, "tensor_copy", ln2b[:, k, :], evtmp[:, sp, :],
          reads=[B(f"evtmp{sp}")], writes=[B("ln2b")])
        for b2 in range(2):
            E(pe, "matmul", psF[:, b2 * 512:(b2 + 1) * 512], gw_sb[:, k, :],
              evtmp[:, sp, b2 * 512:(b2 + 1) * 512],
              start=(k == 0), stop=(k == KD - 1),
              reads=[B("gw_sb"), B(f"evtmp{sp}")], writes=[B("psF")])
        for tc in range(8):
            pgt, pgtb = (psD, "psD") if tc % 2 == 0 else (psE, "psE")
            E(pe, "transpose", pgt[:, 0:128], evtmp[:, sp, tc * 128:(tc + 1) * 128],
              i_f32[:],
              reads=[B(f"evtmp{sp}"), B("i_f32")], writes=[B(pgtb)])
            E(act, "activation", trbf[:, tc, :], pgt[:, 0:128], AF.Copy,
              reads=[B(pgtb)], writes=[B("trbf")])
            ED(so2, hd, ln2T[tc * 128:(tc + 1) * 128, k * 128:(k + 1) * 128],
               trbf[:, tc, :], reads=[B("trbf")], new_group=(tc == 0))

    for tc in range(8):
        E(dve, "tensor_copy", scrow[:], psF[0:16, tc * 128:(tc + 1) * 128],
          reads=[B("psF")], writes=[B("scrow")])
        E(pe, "transpose", psC[:, 0:16], scrow[0:16, 0:128], i_f32[0:16, 0:16],
          reads=[B("scrow"), B("i_f32")], writes=[B("psC")])
        E(dve, "tensor_copy", scT[:], psC[:, 0:16], reads=[B("psC")], writes=[B("scT")])
        E(dve, "reduce_max", mcol[:], scT[:], axis=AX, reads=[B("scT")], writes=[B("mcol")])
        E(dve, "tensor_scalar", prob[:], scT[:], mcol[:, 0:1], None, ALU.subtract,
          reads=[B("scT"), B("mcol")], writes=[B("prob")])
        E(act, "activation", prob[:], prob[:], AF.Exp, reads=[B("prob")], writes=[B("prob")])
        E(dve, "reduce_sum", mcol[:], prob[:], axis=AX, reads=[B("prob")], writes=[B("mcol")])
        E(dve, "reciprocal", mcol[:], mcol[:], reads=[B("mcol")], writes=[B("mcol")])
        E(dve, "tensor_scalar", prob[:], prob[:], mcol[:, 0:1], None, ALU.mult,
          reads=[B("prob"), B("mcol")], writes=[B("prob")])
        E(dve, "max", out=mx8[:], in_=prob[:], reads=[B("prob")], writes=[B("mx8")])
        E(dve, "max_index", out=ix8[:], in_max=mx8[:], in_values=prob[:],
          reads=[B("mx8"), B("prob")], writes=[B("ix8")])
        E(dve, "tensor_reduce", mcol[:], mx8[:, 0:4], axis=AX, op=ALU.add,
          reads=[B("mx8")], writes=[B("mcol")])
        E(dve, "tensor_scalar", mcol[:], mcol[:], 1e-20, None, ALU.add,
          reads=[B("mcol")], writes=[B("mcol")])
        E(dve, "reciprocal", mcol[:], mcol[:], reads=[B("mcol")], writes=[B("mcol")])
        E(dve, "tensor_scalar", w4[:], mx8[:, 0:4], mcol[:, 0:1], None, ALU.mult,
          reads=[B("mx8"), B("mcol")], writes=[B("w4")])
        E(dve, "tensor_copy", idx4f[:], ix8[:, 0:4], reads=[B("ix8")], writes=[B("idx4f")])
        E(dve, "tensor_copy", tk_sb[:, tc, 0:4], idx4f[:], reads=[B("idx4f")], writes=[B("tk_sb")])
        E(dve, "tensor_copy", tk_sb[:, tc, 4:8], w4[:], reads=[B("w4")], writes=[B("tk_sb")])
    ED(so, hd2, topk, tk_sb[:], reads=[B("tk_sb")])

    ED(ldw, sd, swgu_t[:, 0, :, :], swg[0], writes=[B("swgu0")])
    ED(ldw, sd, swgu_t[:, 1, :, :], swu[0],
       writes=[B("swgu1"), B("swgu0")], new_group=False)
    for it in range(NSH):
        pg = 2 * (it % 2)
        pu = pg + 1
        if it + 1 < NSH:
            ng = 2 * ((it + 1) % 2)
            ED(ldw, sd, swgu_t[:, ng, :, :], swg[it + 1],
               writes=[B(f"swgu{ng}")])
            ED(ldw, sd, swgu_t[:, ng + 1, :, :], swu[it + 1],
               writes=[B(f"swgu{ng + 1}"), B(f"swgu{ng}")], new_group=False)
        for b2 in range(2):
            ps, psb = (psA, "psA") if b2 == 0 else (psB, "psB")
            for k in range(KD):
                E(pe, "matmul", ps[:], swgu_t[:, pg, k, :], ln2b[:, k, b2 * 512:(b2 + 1) * 512],
                  start=(k == 0), stop=(k == KD - 1),
                  reads=[B(f"swgu{pg}"), B("ln2b")], writes=[B(psb)])
        for b2 in range(2):
            psg, psgb = (psA, "psA") if b2 == 0 else (psB, "psB")
            ps, psb = (psD, "psD") if b2 == 0 else (psE, "psE")
            E(act, "activation", gactf[:], psg[:], AF.Silu,
              reads=[B(psgb)], writes=[B("gactf")])
            for k in range(KD):
                E(pe, "matmul", ps[:], swgu_t[:, pu, k, :], ln2b[:, k, b2 * 512:(b2 + 1) * 512],
                  start=(k == 0), stop=(k == KD - 1),
                  reads=[B(f"swgu{pu}"), B("ln2b")], writes=[B(psb)])
            E(dve, "tensor_mul", sF[:, it, b2 * 512:(b2 + 1) * 512],
              gactf[:], ps[:],
              reads=[B("gactf"), B(psb)], writes=[B("ynb")])

    ED(ldw, sd, swd_t[:, 0, :, :], swd[0], writes=[B("swd0")])
    for dd in range(KD):
        par = dd % 2
        if dd + 1 < KD:
            ED(ldw, sd, swd_t[:, 1 - par, :, :], swd[dd + 1],
               writes=[B(f"swd{1 - par}")])
        for b2 in range(2):
            ps, psb = (psA, "psA") if b2 == 0 else (psB, "psB")
            for it in range(NSH):
                E(pe, "matmul", ps[:], swd_t[:, par, it, :], sF[:, it, b2 * 512:(b2 + 1) * 512],
                  start=(it == 0), stop=(it == NSH - 1),
                  reads=[B(f"swd{par}"), B("ynb")], writes=[B(psb)])
            E(dve, "tensor_copy", evtmp[:, dd % 2, b2 * 512:(b2 + 1) * 512], ps[:],
              reads=[B(psb)], writes=[B(f"evtmp{dd % 2}")])
        ED(so, hd2, sharedF[dd * 128:(dd + 1) * 128, :], evtmp[:, dd % 2, :],
           reads=[B(f"evtmp{dd % 2}")])

    fw.pool.wait_only(wait=[so.now(), so2.now(), ldu.now(), ldy.now()])
    fw.finalize(es)
    es.close()
    return nc


def build_l3():
    NI = 11
    BIG = float(1 << 20)
    breg = LazyReg(1023)
    from concourse.bass_core_rust import IndirectOffsetOnAxis
    nc = bass.Bass("TRN2", target_bir_lowering=False, debug=False,
                   dynamic_dma_scratch_size=65536)
    fw = Fw(nc)

    ln2g = nc.dram_tensor("ln2g", [8193, D_MODEL], BF16, kind="ExternalInput").ap()
    tkcm = nc.dram_tensor("tkcm", [128, 64, 8], F32, kind="ExternalInput").ap()
    eids = nc.dram_tensor("eids", [1, 2], F32, kind="ExternalInput").ap()
    wg0 = nc.dram_tensor("wg0", [NI, 128, KD, 128], F32, kind="ExternalInput").ap()
    wu0 = nc.dram_tensor("wu0", [NI, 128, KD, 128], F32, kind="ExternalInput").ap()
    wd0 = nc.dram_tensor("wd0", [4, 128, NI, 512], F32, kind="ExternalInput").ap()
    wg1 = nc.dram_tensor("wg1", [NI, 128, KD, 128], F32, kind="ExternalInput").ap()
    wu1 = nc.dram_tensor("wu1", [NI, 128, KD, 128], F32, kind="ExternalInput").ap()
    wd1 = nc.dram_tensor("wd1", [4, 128, NI, 512], F32, kind="ExternalInput").ap()

    eoutT = nc.dram_tensor("eoutT", [2048, D_MODEL], BF16, kind="ExternalOutput").ap()

    gidx0 = nc.dram_tensor("gidx0", [1024, 2], U32, kind="Internal").ap()
    gidx1 = nc.dram_tensor("gidx1", [1024, 2], U32, kind="Internal").ap()

    es = contextlib.ExitStack()
    bufs = {}

    def sb(name, shape, dt=F32):
        t = es.enter_context(nc.sbuf_tensor(name, shape, dt))
        bufs[name] = Buf(name)
        return t

    def pst(name, shape, dt=F32):
        t = es.enter_context(nc.psum_tensor(name, shape, dt))
        bufs[name] = Buf(name)
        return t

    def B(name):
        if name not in bufs:
            bufs[name] = Buf(name)
        return bufs[name]

    tk_sb = sb("tk_sb", [128, 64, 8])
    eid_sb = sb("eid_sb", [1, 2])
    ecol = sb("ecol", [128, 2])
    iota_cm = sb("iota_cm", [128, 64], U32)
    iota_f = sb("iota_f", [128, 64])
    ones1_f = sb("ones1_f", [1, 128])
    ones_col_f = sb("ones_col_f", [128, 1])
    i_f32 = sb("i_f32", [128, 128])
    i_bf = sb("i_bf", [128, 128], BF16)
    texcl = sb("texcl", [128, 128])
    eq4 = sb("eq4", [128, 64, 4])
    m_e = sb("m_e", [128, 64])
    we_e = sb("we_e", [128, 64])
    rank = sb("rank", [128, 64])
    kept = sb("kept", [128, 64])
    sidxf = sb("sidxf", [128, 64])
    sidxu = sb("sidxu", [128, 64], U32)
    pair_t = sb("pair_t", [128, 40, 2], U32)
    pair_t2 = sb("pair_t2", [128, 24, 2], U32)
    prefill = sb("prefill", [128, 8, 2], U32)
    csum_row = sb("csum_row", [1, 64])
    csum_col = sb("csum_col", [128, 1])
    tmp64 = sb("tmp64", [128, 64])
    gwt = sb("gwt", [128, 2, 2], U32)
    wcol = sb("wcol", [128, 2, 8])
    idxcols = sb("idxcols", [128, 8], U32)
    xg_sb = sb("xg_sb", [128, 2, D_MODEL], BF16)
    xgF = sb("xgF", [128, 2, KD, 1024], BF16)
    sFe = sb("sFe", [128, NI, 1024], BF16)
    wgu_t = sb("wgu_t", [128, 4, KD, 128], BF16)
    wd_t = sb("wd_t", [128, 2, NI, 512], BF16)
    gactf = sb("gactf", [128, 512])
    eo_sb = sb("eo_sb", [128, 2, 512], BF16)
    trbf = sb("trbf", [128, 8, 128], BF16)
    scrow = sb("scrow", [16, 128])

    psA = pst("psA", [128, 512])
    psB = pst("psB", [128, 512])
    psD = pst("psD", [128, 512])
    psE = pst("psE", [128, 512])
    psR = pst("psR", [128, 128])
    psG = pst("psG", [128, 128], BF16)
    psH = pst("psH", [128, 128], BF16)

    hd, hd2, sd = fw.hd, fw.hd2, fw.sd
    pe, act, dve, pool = fw.pe, fw.act, fw.dve, fw.pool

    ldt = fw.stream("ldt")
    ldw = fw.stream("ldw")
    gz = fw.stream("gz")
    pf = fw.stream("pf")
    so = fw.stream("so")

    ED(ldt, hd, tk_sb[:], tkcm, writes=[B("tk_sb")])
    ED(ldt, hd, eid_sb[:], eids, writes=[B("eid_sb")])
    E(pool, "iota", iota_cm[:], pattern=[[128, 64]], base=0, channel_multiplier=1,
      writes=[B("iota_cm")])
    E(pool, "memset", i_f32[:], 1.0, writes=[B("i_f32")])
    E(pool, "affine_select", i_f32[:], i_f32[:], pattern=[[-1, 128]],
      compare_op=ALU.is_equal, fill=0.0, base=0, channel_multiplier=1,
      reads=[B("i_f32")], writes=[B("i_f32")])
    E(pool, "memset", texcl[:], 1.0, writes=[B("texcl")])
    E(pool, "affine_select", texcl[:], texcl[:], pattern=[[1, 128]],
      compare_op=ALU.is_gt, fill=0.0, base=0, channel_multiplier=-1,
      reads=[B("texcl")], writes=[B("texcl")])
    E(dve, "tensor_copy", i_bf[:], i_f32[:], reads=[B("i_f32")], writes=[B("i_bf")])
    E(dve, "tensor_copy", iota_f[:], iota_cm[:], reads=[B("iota_cm")], writes=[B("iota_f")])
    E(dve, "memset", ones1_f[:], 1.0, writes=[B("ones1_f")])
    E(dve, "memset", ones_col_f[:], 1.0, writes=[B("ones_col_f")])
    E(dve, "memset", prefill[:, :, 0:1], 8192, writes=[B("prefill")])
    E(dve, "memset", prefill[:, :, 1:2], 0, writes=[B("prefill")])
    E(pe, "matmul", psR[:, 0:2], ones1_f[:], eid_sb[:], start=True, stop=True,
      reads=[B("ones1_f"), B("eid_sb")], writes=[B("psR")])
    E(dve, "tensor_copy", ecol[:], psR[:, 0:2], reads=[B("psR")], writes=[B("ecol")])

    def gen_route(e):
        gidx_d = gidx0 if e == 0 else gidx1
        gidx_b = "gidx0" if e == 0 else "gidx1"

        E(dve, "tensor_scalar", eq4[:], tk_sb[:, :, 0:4], ecol[:, e:e + 1], None,
          ALU.is_equal, reads=[B("tk_sb"), B("ecol")], writes=[B("eq4")])
        E(dve, "tensor_reduce", m_e[:], eq4[:], axis=AX, op=ALU.add,
          reads=[B("eq4")], writes=[B("m_e")])
        E(dve, "tensor_mul", eq4[:], eq4[:], tk_sb[:, :, 4:8],
          reads=[B("eq4"), B("tk_sb")], writes=[B("eq4")])
        E(dve, "tensor_reduce", we_e[:], eq4[:], axis=AX, op=ALU.add,
          reads=[B("eq4")], writes=[B("we_e")])

        E(pe, "matmul", psR[0:1, 0:64], ones_col_f[:], m_e[:], start=True, stop=True,
          reads=[B("ones_col_f"), B("m_e")], writes=[B("psR")])
        E(dve, "tensor_copy", csum_row[:], psR[0:1, 0:64], reads=[B("psR")], writes=[B("csum_row")])
        E(pe, "transpose", psR[0:64, 0:1], csum_row[0:1, 0:64], i_f32[0:1, 0:1],
          reads=[B("csum_row"), B("i_f32")], writes=[B("psR")])
        E(dve, "tensor_copy", csum_col[0:64, 0:1], psR[0:64, 0:1],
          reads=[B("psR")], writes=[B("csum_col")])
        E(pe, "matmul", psR[0:1, 0:64], csum_col[0:64, 0:1], texcl[0:64, 0:64],
          start=True, stop=True,
          reads=[B("csum_col"), B("texcl")], writes=[B("psR")])
        E(dve, "tensor_copy", csum_row[:], psR[0:1, 0:64], reads=[B("psR")], writes=[B("csum_row")])
        E(pe, "matmul", psR[:, 0:64], ones1_f[:], csum_row[0:1, 0:64], start=True, stop=True,
          reads=[B("ones1_f"), B("csum_row")], writes=[B("psR")])
        E(dve, "tensor_copy", rank[:], psR[:, 0:64], reads=[B("psR")], writes=[B("rank")])
        E(pe, "matmul", psR[:, 0:64], texcl[:], m_e[:], start=True, stop=True,
          reads=[B("texcl"), B("m_e")], writes=[B("psR")])
        E(dve, "tensor_add", rank[:], rank[:], psR[:, 0:64],
          reads=[B("rank"), B("psR")], writes=[B("rank")])
        yield

        E(dve, "tensor_scalar", kept[:], rank[:], 1024.0, None, ALU.is_lt,
          reads=[B("rank")], writes=[B("kept")])
        E(dve, "tensor_mul", kept[:], kept[:], m_e[:],
          reads=[B("kept"), B("m_e")], writes=[B("kept")])
        E(dve, "tensor_scalar", sidxf[:], kept[:], -BIG, BIG, ALU.mult, ALU.add,
          reads=[B("kept")], writes=[B("sidxf")])
        E(dve, "tensor_add", sidxf[:], sidxf[:], rank[:],
          reads=[B("sidxf"), B("rank")], writes=[B("sidxf")])
        E(dve, "tensor_copy", sidxu[:], sidxf[:], reads=[B("sidxf")], writes=[B("sidxu")])

        E(dve, "tensor_copy", pair_t[:, :, 0:1].rearrange("p a b -> p (a b)"), iota_cm[:, 0:40],
          reads=[B("iota_cm")], writes=[B("pair_t")])
        E(dve, "tensor_copy", pair_t[:, :, 1:2].rearrange("p a b -> p (a b)"),
          we_e[:, 0:40].bitcast(U32), reads=[B("we_e")], writes=[B("pair_t")])
        E(dve, "tensor_copy", pair_t2[:, :, 0:1].rearrange("p a b -> p (a b)"), iota_cm[:, 40:64],
          reads=[B("iota_cm")], writes=[B("pair_t2")])
        E(dve, "tensor_copy", pair_t2[:, :, 1:2].rearrange("p a b -> p (a b)"),
          we_e[:, 40:64].bitcast(U32), reads=[B("we_e")], writes=[B("pair_t2")])
        ED(pf, hd, gidx_d.rearrange("(p a) b -> p a b", p=128), prefill[:],
           reads=[B("prefill")], writes=[B(gidx_b)])
        yield
        for f in range(64):
            srct = pair_t[:, f, :] if f < 40 else pair_t2[:, f - 40, :]
            EDOP(gz, sd, "indirect_dma_start",
                 out=gidx_d, out_offset=IndirectOffsetOnAxis(sidxu[:, f:f + 1], 0),
                 in_=srct, in_offset=None,
                 bounds_check=breg, oob_is_err=False,
                 reads=[B("sidxu"), B("pair_t"), B("pair_t2"), B(gidx_b)] if f == 0 else [],
                 writes=[] if f < 63 else [B(gidx_b)],
                 new_group=(f % 16 == 0))
            if f % 4 == 3:
                yield

        rtg = 0
        for sl in range(10):
            if sl >= 2:
                so_ = sl - 2
                xpo = so_ % 2
                for k in range(KD):
                    pg_, pgb_ = (psG, "psG") if rtg % 2 == 0 else (psH, "psH")
                    rtg += 1
                    E(pe, "transpose", pg_[:], xg_sb[:, xpo, k * 128:(k + 1) * 128], i_bf[:],
                      reads=[B(f"xg{xpo}"), B("i_bf")], writes=[B(pgb_)])
                    E(act, "activation", xgF[:, e, k, so_ * 128:(so_ + 1) * 128], pg_[:],
                      AF.Copy, reads=[B(pgb_)], writes=[B(f"xgF{e}")])
            if sl < 8:
                xp = sl % 2
                ED(ldt, hd, gwt[:, xp, :], gidx_d[sl * 128:(sl + 1) * 128, :],
                   reads=[B(gidx_b)], writes=[B(f"gwt{xp}")])
                E(dve, "tensor_copy", idxcols[:, sl:sl + 1], gwt[:, xp, 0:1],
                  reads=[B(f"gwt{xp}")], writes=[B("idxcols")])
                E(dve, "tensor_copy", wcol[:, e, sl:sl + 1].bitcast(U32), gwt[:, xp, 1:2],
                  reads=[B(f"gwt{xp}")], writes=[B(f"wcol{e}")])
                EDOP(gz, sd, "indirect_dma_start",
                     out=xg_sb[:, xp, :], out_offset=None,
                     in_=ln2g, in_offset=IndirectOffsetOnAxis(idxcols[:, sl:sl + 1], 0),
                     reads=[B("idxcols")], writes=[B(f"xg{xp}")])
            yield

    def gen_ffn(e):
        wgD = wg0 if e == 0 else wg1
        wuD = wu0 if e == 0 else wu1
        wdD = wd0 if e == 0 else wd1

        ED(ldw, sd, wgu_t[:, 0, :, :], wgD[0], writes=[B("wgu0")])
        ED(ldw, sd, wgu_t[:, 1, :, :], wuD[0],
           writes=[B("wgu1"), B("wgu0")], new_group=False)
        for it in range(NI):
            pg = 2 * (it % 2)
            pu = pg + 1
            if it + 1 < NI:
                ng = 2 * ((it + 1) % 2)
                ED(ldw, sd, wgu_t[:, ng, :, :], wgD[it + 1],
                   writes=[B(f"wgu{ng}")])
                ED(ldw, sd, wgu_t[:, ng + 1, :, :], wuD[it + 1],
                   writes=[B(f"wgu{ng + 1}"), B(f"wgu{ng}")], new_group=False)
            for b2 in range(2):
                ps, psb = (psA, "psA") if b2 == 0 else (psB, "psB")
                for k in range(KD):
                    E(pe, "matmul", ps[:], wgu_t[:, pg, k, :],
                      xgF[:, e, k, b2 * 512:(b2 + 1) * 512],
                      start=(k == 0), stop=(k == KD - 1),
                      reads=[B(f"wgu{pg}"), B(f"xgF{e}")], writes=[B(psb)])
            yield
            for b2 in range(2):
                psg, psgb = (psA, "psA") if b2 == 0 else (psB, "psB")
                ps, psb = (psD, "psD") if b2 == 0 else (psE, "psE")
                E(act, "activation", gactf[:], psg[:], AF.Silu,
                  reads=[B(psgb)], writes=[B("gactf")])
                for k in range(KD):
                    E(pe, "matmul", ps[:], wgu_t[:, pu, k, :],
                      xgF[:, e, k, b2 * 512:(b2 + 1) * 512],
                      start=(k == 0), stop=(k == KD - 1),
                      reads=[B(f"wgu{pu}"), B(f"xgF{e}")], writes=[B(psb)])
                E(dve, "tensor_mul", sFe[:, it, b2 * 512:(b2 + 1) * 512], gactf[:], ps[:],
                  reads=[B("gactf"), B(psb)], writes=[B("sFe")])
            yield

        ED(ldw, sd, wd_t[:, 0, :, :], wdD[0], writes=[B("wd0")])
        for dd4 in range(4):
            wp = dd4 % 2
            if dd4 + 1 < 4:
                ED(ldw, sd, wd_t[:, 1 - wp, :, :], wdD[dd4 + 1],
                   writes=[B(f"wd{1 - wp}")])
            for sl in range(8):
                ps, psb = (psA, "psA") if sl % 2 == 0 else (psB, "psB")
                ep = sl % 2
                for it in range(NI):
                    E(pe, "matmul", ps[:], sFe[:, it, sl * 128:(sl + 1) * 128],
                      wd_t[:, wp, it, :],
                      start=(it == 0), stop=(it == NI - 1),
                      reads=[B("sFe"), B(f"wd{wp}")], writes=[B(psb)])
                E(dve, "tensor_scalar", eo_sb[:, ep, :], ps[:],
                  wcol[:, e, sl:sl + 1], None, ALU.mult,
                  reads=[B(psb), B(f"wcol{e}")], writes=[B(f"eo{ep}")])
                ED(so, hd2,
                   eoutT[e * 1024 + sl * 128:e * 1024 + (sl + 1) * 128,
                         dd4 * 512:(dd4 + 1) * 512],
                   eo_sb[:, ep, :], reads=[B(f"eo{ep}")])
                if sl % 2 == 1:
                    yield

    for _ in gen_route(0):
        pass
    gf = gen_ffn(0)
    gr = gen_route(1)
    df = dr = False
    fi = 0
    while not (df and dr):
        if not df:
            try:
                next(gf)
                fi += 1
            except StopIteration:
                df = True
        if not dr and (fi >= 6 or df):
            try:
                next(gr)
            except StopIteration:
                dr = True
    for _ in gen_ffn(1):
        pass

    fw.pool.wait_only(wait=[so.now()])
    fw.finalize(es)
    es.close()
    return nc


def build_l4():
    from concourse.bass_core_rust import IndirectOffsetOnAxis
    nc = bass.Bass("TRN2", target_bir_lowering=False, debug=False)
    fw = Fw(nc)

    eoutg = nc.dram_tensor("eoutg", [16385, D_MODEL], BF16, kind="ExternalInput").ap()
    tkrot = nc.dram_tensor("tkrot", [128, 64, 8], F32, kind="ExternalInput").ap()
    wrapm = nc.dram_tensor("wrapm", [1, 64], F32, kind="ExternalInput").ap()
    x1F_sl = nc.dram_tensor("x1F_sl", [D_MODEL, 1024], F32, kind="ExternalInput").ap()
    shF_sl = nc.dram_tensor("shF_sl", [D_MODEL, 1024], F32, kind="ExternalInput").ap()

    outsl = nc.dram_tensor("outsl", [1024, D_MODEL], F32, kind="ExternalOutput").ap()

    es = contextlib.ExitStack()
    bufs = {}

    def sb(name, shape, dt=F32):
        t = es.enter_context(nc.sbuf_tensor(name, shape, dt))
        bufs[name] = Buf(name)
        return t

    def pst(name, shape, dt=F32):
        t = es.enter_context(nc.psum_tensor(name, shape, dt))
        bufs[name] = Buf(name)
        return t

    def B(name):
        if name not in bufs:
            bufs[name] = Buf(name)
        return bufs[name]

    tk_sb = sb("tk_sb", [128, 64, 8])
    wm_sb = sb("wm_sb", [1, 64])
    ones1_f = sb("ones1_f", [1, 128])
    ones_col_f = sb("ones_col_f", [128, 1])
    i_f32 = sb("i_f32", [128, 128])
    i_bf = sb("i_bf", [128, 128], BF16)
    texcl = sb("texcl", [128, 128])
    epsc = sb("epsc", [128, 1])
    eq4 = sb("eq4", [128, 64, 4])
    m_e = sb("m_e", [128, 64])
    csum_row = sb("csum_row", [1, 64])
    csum_col = sb("csum_col", [128, 1])
    nat_row = sb("nat_row", [1, 64])
    tot_c = sb("tot_c", [1, 1])
    cc_c = sb("cc_c", [1, 1])
    trow = sb("trow", [1, 64])
    rankm = sb("rankm", [128, 8])
    keptm = sb("keptm", [128, 8])
    rowsel = sb("rowsel", [128, 8])
    t8a = sb("t8a", [128, 8])
    grow = sb("grow", [128, 8, 4])
    keptk = sb("keptk", [128, 8, 4])
    gidxu = sb("gidxu", [128, 8, 4], U32)
    x1rows = sb("x1rows", [128, 2, D_MODEL])
    x1tiles = sb("x1tiles", [128, 2, KD, 128])
    shtiles = sb("shtiles", [128, 2, KD, 128])
    shrows = sb("shrows", [128, 2, D_MODEL], BF16)
    shbf = sb("shbf", [128, 2, D_MODEL], BF16)
    ygt = sb("ygt", [128, 2, D_MODEL], BF16)
    ygt2 = sb("ygt2", [128, 2, D_MODEL], BF16)
    ffs = sb("ffs", [128, 2, D_MODEL])
    sq = sb("sq", [128, 2, D_MODEL], BF16)
    mcol = sb("mcol", [128, 2])
    vcol = sb("vcol", [128, 2])
    rcol = sb("rcol", [128, 2])
    outt = sb("outt", [128, 2, D_MODEL])

    psR = pst("psR", [128, 128])
    psG = pst("psG", [128, 128], BF16)
    psT = pst("psT", [128, 128])
    psT2 = pst("psT2", [128, 128])

    hd, hd2, sd = fw.hd, fw.hd2, fw.sd
    pe, act, dve, pool = fw.pe, fw.act, fw.dve, fw.pool

    ldt = fw.stream("ldt")
    ldx = fw.stream("ldx")
    ldx2 = fw.stream("ldx2")
    gza = fw.stream("gza")
    gzb = fw.stream("gzb")
    so = fw.stream("so")

    ED(ldt, hd, tk_sb[:], tkrot, writes=[B("tk_sb")])
    ED(ldt, hd, wm_sb[:], wrapm, writes=[B("wm_sb")])
    E(pool, "memset", i_f32[:], 1.0, writes=[B("i_f32")])
    E(pool, "affine_select", i_f32[:], i_f32[:], pattern=[[-1, 128]],
      compare_op=ALU.is_equal, fill=0.0, base=0, channel_multiplier=1,
      reads=[B("i_f32")], writes=[B("i_f32")])
    E(pool, "memset", texcl[:], 1.0, writes=[B("texcl")])
    E(pool, "affine_select", texcl[:], texcl[:], pattern=[[1, 128]],
      compare_op=ALU.is_gt, fill=0.0, base=0, channel_multiplier=-1,
      reads=[B("texcl")], writes=[B("texcl")])
    E(dve, "tensor_copy", i_bf[:], i_f32[:], reads=[B("i_f32")], writes=[B("i_bf")])
    E(dve, "memset", ones1_f[:], 1.0, writes=[B("ones1_f")])
    E(dve, "memset", ones_col_f[:], 1.0, writes=[B("ones_col_f")])
    E(dve, "memset", epsc[:], LN_EPS, writes=[B("epsc")])
    E(dve, "memset", grow[:], 0.0, writes=[B("grow")])
    E(dve, "memset", keptk[:], 0.0, writes=[B("keptk")])

    for e in range(16):
        psR_e, psR_b = (psR, "psR") if e % 2 == 0 else (psT, "psT")
        E(dve, "tensor_scalar", eq4[:], tk_sb[:, :, 0:4], float(e), None,
          ALU.is_equal, reads=[B("tk_sb")], writes=[B("eq4")])
        E(dve, "tensor_reduce", m_e[:], eq4[:], axis=AX, op=ALU.add,
          reads=[B("eq4")], writes=[B("m_e")])
        E(pe, "matmul", psR_e[0:1, 0:64], ones_col_f[:], m_e[:], start=True, stop=True,
          reads=[B("ones_col_f"), B("m_e")], writes=[B(psR_b)])
        E(dve, "tensor_copy", csum_row[:], psR_e[0:1, 0:64], reads=[B(psR_b)], writes=[B("csum_row")])
        E(pe, "transpose", psR_e[0:64, 0:1], csum_row[0:1, 0:64], i_f32[0:1, 0:1],
          reads=[B("csum_row"), B("i_f32")], writes=[B(psR_b)])
        E(dve, "tensor_copy", csum_col[0:64, 0:1], psR_e[0:64, 0:1],
          reads=[B(psR_b)], writes=[B("csum_col")])
        E(pe, "matmul", psR_e[0:1, 0:64], csum_col[0:64, 0:1], texcl[0:64, 0:64],
          start=True, stop=True,
          reads=[B("csum_col"), B("texcl")], writes=[B(psR_b)])
        E(dve, "tensor_copy", nat_row[:], psR_e[0:1, 0:64], reads=[B(psR_b)], writes=[B("nat_row")])
        E(dve, "tensor_reduce", tot_c[:], csum_row[:], axis=AX, op=ALU.add,
          reads=[B("csum_row")], writes=[B("tot_c")])
        E(dve, "tensor_mul", trow[:], csum_row[:], wm_sb[:],
          reads=[B("csum_row"), B("wm_sb")], writes=[B("trow")])
        E(dve, "tensor_reduce", cc_c[:], trow[:], axis=AX, op=ALU.add,
          reads=[B("trow")], writes=[B("cc_c")])
        E(dve, "tensor_scalar", trow[:], wm_sb[:], tot_c[0:1, 0:1], None, ALU.mult,
          reads=[B("wm_sb"), B("tot_c")], writes=[B("trow")])
        E(dve, "tensor_sub", nat_row[:], nat_row[:], trow[:],
          reads=[B("nat_row"), B("trow")], writes=[B("nat_row")])
        E(dve, "tensor_scalar", nat_row[:], nat_row[:], cc_c[0:1, 0:1], None, ALU.add,
          reads=[B("nat_row"), B("cc_c")], writes=[B("nat_row")])
        E(pe, "matmul", psR_e[:, 0:8], ones1_f[:], nat_row[0:1, 0:8], start=True, stop=True,
          reads=[B("ones1_f"), B("nat_row")], writes=[B(psR_b)])
        E(dve, "tensor_copy", rankm[:], psR_e[:, 0:8], reads=[B(psR_b)], writes=[B("rankm")])
        E(pe, "matmul", psR_e[:, 0:8], texcl[:], m_e[:, 0:8], start=True, stop=True,
          reads=[B("texcl"), B("m_e")], writes=[B(psR_b)])
        E(dve, "tensor_add", rankm[:], rankm[:], psR_e[:, 0:8],
          reads=[B("rankm"), B(psR_b)], writes=[B("rankm")])
        E(dve, "tensor_scalar", keptm[:], rankm[:], 1024.0, None, ALU.is_lt,
          reads=[B("rankm")], writes=[B("keptm")])
        E(dve, "tensor_mul", keptm[:], keptm[:], m_e[:, 0:8],
          reads=[B("keptm"), B("m_e")], writes=[B("keptm")])
        for k in range(4):
            E(dve, "tensor_mul", rowsel[:], eq4[:, 0:8, k].rearrange("p a -> p a"), keptm[:],
              reads=[B("eq4"), B("keptm")], writes=[B("rowsel")])
            E(dve, "tensor_scalar", t8a[:], rankm[:], 1024.0 * e, None, ALU.add,
              reads=[B("rankm")], writes=[B("t8a")])
            E(dve, "tensor_mul", t8a[:], t8a[:], rowsel[:],
              reads=[B("t8a"), B("rowsel")], writes=[B("t8a")])
            E(dve, "tensor_add", grow[:, :, k], grow[:, :, k], t8a[:],
              reads=[B("grow"), B("t8a")], writes=[B("grow")])
            E(dve, "tensor_add", keptk[:, :, k], keptk[:, :, k], rowsel[:],
              reads=[B("keptk"), B("rowsel")], writes=[B("keptk")])

    for k in range(4):
        E(dve, "tensor_scalar", rowsel[:], keptk[:, :, k], -16384.0, 16384.0,
          ALU.mult, ALU.add, reads=[B("keptk")], writes=[B("rowsel")])
        E(dve, "tensor_add", rowsel[:], rowsel[:], grow[:, :, k],
          reads=[B("rowsel"), B("grow")], writes=[B("rowsel")])
        E(dve, "tensor_copy", gidxu[:, :, k], rowsel[:],
          reads=[B("rowsel")], writes=[B("gidxu")])

    for c in range(8):
        cp = c % 2
        for k in range(2):
            EDOP(gza, sd, "indirect_dma_start",
                 out=ygt[:, cp, :], out_offset=None,
                 in_=eoutg, in_offset=IndirectOffsetOnAxis(gidxu[:, c, k:k + 1], 0),
                 compute_op=(ALU.bypass if k == 0 else ALU.add),
                 reads=[B("gidxu")] + ([B(f"ygt{cp}")] if k > 0 else []),
                 writes=[B(f"ygt{cp}")])
        for k in range(2, 4):
            EDOP(gzb, sd, "indirect_dma_start",
                 out=ygt2[:, cp, :], out_offset=None,
                 in_=eoutg, in_offset=IndirectOffsetOnAxis(gidxu[:, c, k:k + 1], 0),
                 compute_op=(ALU.bypass if k == 2 else ALU.add),
                 reads=[B("gidxu")] + ([B(f"ygt2{cp}")] if k > 2 else []),
                 writes=[B(f"ygt2{cp}")])
        ED(ldx, hd, shtiles[:, cp, :, :],
           shF_sl[:, c * 128:(c + 1) * 128].rearrange("(kk p) t -> p kk t", p=128),
           writes=[B(f"shtiles{cp}")])
        ED(ldx2, hd2, x1tiles[:, cp, :, :],
           x1F_sl[:, c * 128:(c + 1) * 128].rearrange("(kk p) t -> p kk t", p=128),
           writes=[B(f"x1tiles{cp}")])
        for kk in range(KD):
            E(dve, "tensor_copy", shbf[:, cp, kk * 128:(kk + 1) * 128], shtiles[:, cp, kk, :],
              reads=[B(f"shtiles{cp}")], writes=[B(f"shbf{cp}")])
        for kk in range(KD):
            E(pe, "transpose", psG[:], shbf[:, cp, kk * 128:(kk + 1) * 128], i_bf[:],
              reads=[B(f"shbf{cp}"), B("i_bf")], writes=[B("psG")])
            E(dve, "tensor_copy", shrows[:, cp, kk * 128:(kk + 1) * 128], psG[:],
              reads=[B("psG")], writes=[B(f"shrows{cp}")])
            E(pe, "transpose", psT2[:], x1tiles[:, cp, kk, :], i_f32[:],
              reads=[B(f"x1tiles{cp}"), B("i_f32")], writes=[B("psT2")])
            E(dve, "tensor_copy", x1rows[:, cp, kk * 128:(kk + 1) * 128], psT2[:],
              reads=[B("psT2")], writes=[B(f"x1rows{cp}")])
        E(dve, "tensor_add", ffs[:, cp, :], ygt[:, cp, :], ygt2[:, cp, :],
          reads=[B(f"ygt{cp}"), B(f"ygt2{cp}")], writes=[B(f"ffs{cp}")])
        E(dve, "tensor_add", ffs[:, cp, :], ffs[:, cp, :], shrows[:, cp, :],
          reads=[B(f"ffs{cp}"), B(f"shrows{cp}")], writes=[B(f"ffs{cp}")])
        E(dve, "tensor_reduce", mcol[:, cp:cp + 1], ffs[:, cp, :], axis=AX, op=ALU.add,
          reads=[B(f"ffs{cp}")], writes=[B(f"mcol{cp}")])
        E(dve, "tensor_scalar_mul", mcol[:, cp:cp + 1], mcol[:, cp:cp + 1], 1.0 / D_MODEL,
          reads=[B(f"mcol{cp}")], writes=[B(f"mcol{cp}")])
        E(act, "activation", sq[:, cp, :], ffs[:, cp, :], AF.Square,
          reads=[B(f"ffs{cp}")], writes=[B(f"sq{cp}")])
        E(dve, "tensor_reduce", vcol[:, cp:cp + 1], sq[:, cp, :], axis=AX, op=ALU.add,
          reads=[B(f"sq{cp}")], writes=[B(f"vcol{cp}")])
        E(dve, "tensor_scalar_mul", vcol[:, cp:cp + 1], vcol[:, cp:cp + 1], 1.0 / D_MODEL,
          reads=[B(f"vcol{cp}")], writes=[B(f"vcol{cp}")])
        E(dve, "tensor_mul", rcol[:, cp:cp + 1], mcol[:, cp:cp + 1], mcol[:, cp:cp + 1],
          reads=[B(f"mcol{cp}")], writes=[B(f"rcol{cp}")])
        E(dve, "tensor_sub", vcol[:, cp:cp + 1], vcol[:, cp:cp + 1], rcol[:, cp:cp + 1],
          reads=[B(f"vcol{cp}"), B(f"rcol{cp}")], writes=[B(f"vcol{cp}")])
        E(act, "activation", rcol[:, cp:cp + 1], vcol[:, cp:cp + 1], AF.Ln, bias=epsc[:, 0:1],
          reads=[B(f"vcol{cp}"), B("epsc")], writes=[B(f"rcol{cp}")])
        E(act, "activation", rcol[:, cp:cp + 1], rcol[:, cp:cp + 1], AF.Exp, scale=-0.5,
          reads=[B(f"rcol{cp}")], writes=[B(f"rcol{cp}")])
        E(dve, "tensor_scalar", ffs[:, cp, :], ffs[:, cp, :], mcol[:, cp:cp + 1], None,
          ALU.subtract,
          reads=[B(f"ffs{cp}"), B(f"mcol{cp}")], writes=[B(f"ffs{cp}")])
        E(dve, "tensor_scalar", ffs[:, cp, :], ffs[:, cp, :], rcol[:, cp:cp + 1], None,
          ALU.mult,
          reads=[B(f"ffs{cp}"), B(f"rcol{cp}")], writes=[B(f"ffs{cp}")])
        E(dve, "tensor_add", outt[:, cp, :], ffs[:, cp, :], x1rows[:, cp, :],
          reads=[B(f"ffs{cp}"), B(f"x1rows{cp}")], writes=[B(f"outt{cp}")])
        ED(so, hd2, outsl[c * 128:(c + 1) * 128, :], outt[:, cp, :],
           reads=[B(f"outt{cp}")])

    fw.pool.wait_only(wait=[so.now()])
    fw.finalize(es)
    es.close()
    return nc


D_MODEL = 2048
D_INNER = 4096
NHEADS = 32
HL = 4
NC = 8
DSTATE = 64


def l1_inputs(inp, seq=8192):
    u = np.ascontiguousarray(inp["L0_attn_u"][0, :seq, :])
    uT = np.ascontiguousarray(u.T)
    W = inp["L0_attn_in_proj_w"]
    cw = inp["L0_attn_conv_w"]
    cb = inp["L0_attn_conv_b"]
    maps = []
    for i in range(NC):
        hs = [4 * i + j for j in range(HL)]
        z_rows = np.concatenate([W[128 * h:128 * h + 128] for h in hs], 0)
        x_rows = np.concatenate([W[D_INNER + 128 * h:D_INNER + 128 * h + 128] for h in hs], 0)
        bc_rows = W[2 * D_INNER:2 * D_INNER + 2 * DSTATE]
        dt_rows = W[2 * D_INNER + 2 * DSTATE + 4 * i:2 * D_INNER + 2 * DSTATE + 4 * i + HL]
        pad = np.zeros((128 - HL, D_MODEL), np.float32)
        wsl = np.concatenate([z_rows, x_rows, bc_rows, dt_rows, pad], 0)
        wT = np.ascontiguousarray(wsl.T)

        cwx = np.concatenate([cw[128 * h:128 * h + 128] for h in hs] +
                             [cw[D_INNER:D_INNER + 2 * DSTATE]], 0)
        convw = np.ascontiguousarray(cwx.reshape(5, 128, 4).transpose(1, 0, 2))
        cbx = np.concatenate([cb[128 * h:128 * h + 128] for h in hs] +
                             [cb[D_INNER:D_INNER + 2 * DSTATE]], 0)
        convb = np.ascontiguousarray(cbx.reshape(5, 128).T)

        alog = np.zeros((32, 1), np.float32)
        alog[0:HL, 0] = inp["L0_attn_A_log"][4 * i:4 * i + HL]
        dtb = np.zeros((32, 1), np.float32)
        dtb[0:HL, 0] = inp["L0_attn_dt_bias"][4 * i:4 * i + HL]
        dskr = np.ascontiguousarray(inp["L0_attn_D_skip"][4 * i:4 * i + HL].reshape(1, HL))
        nwc = np.ascontiguousarray(
            inp["L0_attn_norm_w"][512 * i:512 * i + 512].reshape(HL, 128).T)

        maps.append(dict(uT=uT, wT=wT, convw=convw, convb=convb,
                         alog=alog, dtb=dtb, dskr=dskr, nwc=nwc))
    return maps


def l2_inputs(inp, yn_full_F, seq=8192):
    u = np.ascontiguousarray(inp["L0_attn_u"][0, :seq, :])
    uT = np.ascontiguousarray(u.T)
    opT0 = inp["L0_attn_o_proj_w"].T
    opT = np.ascontiguousarray(
        opT0.reshape(2, 16, 128, 16, 128).transpose(3, 0, 2, 1, 4))
    gwT = np.ascontiguousarray(inp["L0_ffn_gate_w"].T)
    swg = np.ascontiguousarray(
        inp["L0_ffn_sw_gate"].reshape(16, 128, 22, 128).transpose(2, 1, 0, 3))
    swu = np.ascontiguousarray(
        inp["L0_ffn_sw_up"].reshape(16, 128, 22, 128).transpose(2, 1, 0, 3))
    swd = np.ascontiguousarray(
        inp["L0_ffn_sw_down"].reshape(22, 128, 16, 128).transpose(2, 1, 0, 3))
    ncr = seq // 1024
    maps = []
    for i in range(ncr):
        maps.append(dict(
            ynF_sl=np.ascontiguousarray(yn_full_F[:, 1024 * i:1024 * (i + 1)]),
            opT=opT,
            uT_sl=np.ascontiguousarray(uT[:, 1024 * i:1024 * (i + 1)]),
            gwT=gwT, swg=swg, swu=swu, swd=swd))
    return maps


def topk_cm_from_fields(topk_i, topk_w, seq=8192):
    nchunks = seq // 128
    A = np.zeros((128, nchunks, 8), np.float32)
    ti = np.asarray(topk_i).reshape(seq, 4)
    tw = np.asarray(topk_w).reshape(seq, 4)
    for f in range(nchunks):
        A[:, f, 0:4] = ti[f * 128:(f + 1) * 128]
        A[:, f, 4:8] = tw[f * 128:(f + 1) * 128]
    return np.ascontiguousarray(A)




def _pre_gu(w):
    return np.ascontiguousarray(w.reshape(16, 128, 11, 128).transpose(2, 1, 0, 3))


def _pre_dn(w):
    return np.ascontiguousarray(w.reshape(11, 128, 4, 512).transpose(2, 1, 0, 3))


def l3_inputs(inp, ln2_rows_bf16, tkcm):
    maps = []
    for i in range(NC):
        e0, e1 = 2 * i, 2 * i + 1
        maps.append(dict(
            ln2g=ln2_rows_bf16,
            tkcm=tkcm,
            eids=np.array([[e0, e1]], np.float32),
            wg0=_pre_gu(inp["L0_ffn_w_gate"][e0]),
            wu0=_pre_gu(inp["L0_ffn_w_up"][e0]),
            wd0=_pre_dn(inp["L0_ffn_w_down"][e0]),
            wg1=_pre_gu(inp["L0_ffn_w_gate"][e1]),
            wu1=_pre_gu(inp["L0_ffn_w_up"][e1]),
            wd1=_pre_dn(inp["L0_ffn_w_down"][e1]),
        ))
    return maps


def l4_inputs(inp, eout_all_bf16, topk_i, topk_w, x1F_parts, shF_parts):
    tkcm = topk_cm_from_fields(topk_i, topk_w)
    maps = []
    for i in range(NC):
        r = 8 * i
        tkrot = np.ascontiguousarray(np.roll(tkcm, -r, axis=1))
        wrapm = np.zeros((1, 64), np.float32)
        if r > 0:
            wrapm[0, 64 - r:] = 1.0
        maps.append(dict(
            eoutg=eout_all_bf16,
            tkrot=tkrot,
            wrapm=wrapm,
            x1F_sl=x1F_parts[i],
            shF_sl=shF_parts[i],
        ))
    return maps



def _l4_inputs_tkcm(eout_all_bf16, tkcm, x1F_parts, shF_parts):
    maps = []
    for i in range(NC):
        r = 8 * i
        tkrot = np.ascontiguousarray(np.roll(tkcm, -r, axis=1))
        wrapm = np.zeros((1, 64), np.float32)
        if r > 0:
            wrapm[0, 64 - r:] = 1.0
        maps.append(dict(eoutg=eout_all_bf16, tkrot=tkrot, wrapm=wrapm,
                         x1F_sl=x1F_parts[i], shF_sl=shF_parts[i]))
    return maps


_NC_CACHE = {}


def _get_programs():
    if "l1" not in _NC_CACHE:
        _NC_CACHE["l1"] = build_l1(seq=8192)
        _NC_CACHE["l2"] = build_l2()
        _NC_CACHE["l3"] = build_l3()
        _NC_CACHE["l4"] = build_l4()
    return _NC_CACHE["l1"], _NC_CACHE["l2"], _NC_CACHE["l3"], _NC_CACHE["l4"]


def kernel(**inputs):
    import ml_dtypes
    from concourse.bass_utils import run_bass_kernel_spmd
    inp = {k: np.asarray(v) for k, v in inputs.items()}
    cores = list(range(NC))
    nc1, nc2, nc3, nc4 = _get_programs()

    maps1 = l1_inputs(inp, seq=8192)
    r1 = run_bass_kernel_spmd(nc1, maps1, core_ids=cores).results
    ynF = np.concatenate([np.asarray(r["ynF"]) for r in r1], axis=0)

    maps2 = l2_inputs(inp, ynF, seq=8192)
    r2 = run_bass_kernel_spmd(nc2, maps2, core_ids=cores).results
    x1F_parts = [np.asarray(r["x1F"]) for r in r2]
    shF_parts = [np.asarray(r["sharedF"]) for r in r2]
    ln2_rows = np.concatenate([np.asarray(r["ln2T"]) for r in r2], axis=0)
    ln2g = np.concatenate(
        [ln2_rows, np.zeros((1, D_MODEL), ml_dtypes.bfloat16)], axis=0)
    tkcm = np.concatenate([np.asarray(r["topk"]) for r in r2], axis=1)
    tkcm = np.ascontiguousarray(tkcm)

    maps3 = l3_inputs(inp, ln2g, tkcm)
    r3 = run_bass_kernel_spmd(nc3, maps3, core_ids=cores).results
    eout = np.concatenate([np.asarray(r["eoutT"]) for r in r3], axis=0)
    eoutg = np.concatenate(
        [eout, np.zeros((1, D_MODEL), ml_dtypes.bfloat16)], axis=0)

    maps4 = _l4_inputs_tkcm(eoutg, tkcm, x1F_parts, shF_parts)
    r4 = run_bass_kernel_spmd(nc4, maps4, core_ids=cores).results
    out = np.concatenate([np.asarray(r["outsl"]) for r in r4], axis=0)
    return out[None, :, :].astype(np.float32)
